# version 07 (final)
# speedup vs baseline: 1.1526x; 1.0336x over previous
import contextlib
import math
import numpy as np
import ml_dtypes
import concourse.bass as bass
import concourse.mybir as mybir
from concourse.bass_utils import run_bass_kernel_spmd

F32 = mybir.dt.float32
BF16 = mybir.dt.bfloat16
U8 = mybir.dt.uint8
AF = mybir.ActivationFunctionType
ALU = mybir.AluOpType
AX = mybir.AxisListType

NCORES = 8
T = 16384
TC = T // NCORES
D = 1024
DT = D // 128
EXP = 1536
EXPT = EXP // 128
I0 = 2816
I0T = 22
MLPT = 32
HEADS = 16
HD = 64
NEXP = 8
CAPACITY = max(1, math.ceil(1.25 * (2 * T) / NEXP))
EPS_LN = 1e-5
EPS_MOE = 1.1920929e-07
QMAX = 127.0
MAGIC = 12582912.0
RG = [list(range(NCORES))]
NTT = TC // 128

BF = ml_dtypes.bfloat16


class _Call:
    __slots__ = ("method", "args", "kwargs", "incs", "own_count")

    def __init__(self, method, args, kwargs):
        self.method = method
        self.args = args
        self.kwargs = kwargs
        self.incs = []
        self.own_count = None


class _CallHandle:
    __slots__ = ("call",)

    def __init__(self, call):
        self.call = call

    def then_inc(self, sem, val=1):
        self.call.incs.append((sem, val))
        return self


class Q:

    def __init__(self):
        self.calls = []

    def rec(self, name, *args, **kwargs):
        c = _Call(name, args, kwargs)
        self.calls.append(c)
        return _CallHandle(c)

    def __getattr__(self, name):
        if name.startswith("_"):
            raise AttributeError(name)
        return lambda *a, **k: self.rec(name, *a, **k)

    def replay(self, eng):
        for c in self.calls:
            ins = getattr(eng, c.method)(*c.args, **c.kwargs)
            for sem, val in c.incs:
                ins = ins.then_inc(sem, val)


ENGINES = ("sync", "gpsimd", "tensor", "vector", "scalar")
ESEM = {"sync": "syE", "gpsimd": "gE", "tensor": "tE", "vector": "vE", "scalar": "sE"}
DSEM = {"sync": "dS", "gpsimd": "dG", "scalar": "dC"}


class Prog:
    def __init__(self, nc):
        self.nc = nc
        self.q = {e: Q() for e in ENGINES}
        self.es = contextlib.ExitStack()
        self.sems = {}
        self.cnt = {}
        for s in ("dS", "dG", "dC", "dP", "tE", "vE", "sE", "gE", "syE", "cc", "bar"):
            self.sems[s] = self.es.enter_context(nc.semaphore(s))
            self.cnt[s] = 0
        self._bar_target = 0
        self._psum = [self.es.enter_context(nc.psum_tensor(f"psb{i}", [128, 512], F32))
                      for i in range(7)]
        self._junk = {e: self.es.enter_context(nc.sbuf_tensor(f"junk_{e}", [1, 2], F32))
                      for e in ("vector", "scalar", "gpsimd")}
        self._psum_bf = self.es.enter_context(nc.psum_tensor("psbf", [128, 1024], BF16))

    def sb(self, name, shape, dtype=F32, stack=None):
        return (stack or self.es).enter_context(self.nc.sbuf_tensor(name, shape, dtype))

    def ps(self, i):
        return self._psum[i]

    @property
    def psbf(self):
        return self._psum_bf

    def bump(self, s, v=None):
        if v is None:
            v = 16 if (s in DSEM.values() or s == "dP") else 1
        self.cnt[s] += v
        return self.sems[s], v

    def val(self, s):
        return self.cnt[s]

    def wait(self, eng, s, val=None):
        if val is None:
            val = self.cnt[s]
        if val > 0:
            self.q[eng].wait_ge(self.sems[s], val)

    def _junk_op(self, eng):
        q = self.q[eng]
        if eng == "vector":
            return q.rec("tensor_copy", self._junk["vector"][0:1, 0:1], self._junk["vector"][0:1, 1:2])
        if eng == "scalar":
            return q.rec("copy", self._junk["scalar"][0:1, 0:1], self._junk["scalar"][0:1, 1:2])
        if eng == "gpsimd":
            return q.rec("memset", self._junk["gpsimd"][0:1, 0:1], 0.0)
        raise RuntimeError(f"no junk op for {eng}")

    def sig(self, eng, sem=None):
        sem = sem or ESEM[eng]
        semh = self.sems[sem]
        q = self.q[eng]
        for c in reversed(q.calls):
            if c.method in ("wait_ge", "nop"):
                continue
            if any(s2 is semh for s2, _ in c.incs):
                return c.own_count
            if not c.incs:
                _CallHandle(c).then_inc(*self.bump(sem))
                c.own_count = self.cnt[sem]
                return c.own_count
            if eng == "tensor":
                raise RuntimeError("tensor op carries foreign inc; cannot sig")
            cj = self._junk_op(eng)
            cj.then_inc(*self.bump(sem))
            cj.call.own_count = self.cnt[sem]
            return cj.call.own_count
        return 0

    def edge(self, src, dst):
        n = self.sig(src)
        self.wait(dst, ESEM[src], n)

    def vsync(self, eng="vector"):
        n = self.sig(eng)
        self.wait(eng, ESEM[eng], n)

    def dma(self, q, out, in_):
        sem = DSEM[q]
        h = self.q[q].rec("dma_start", out=out, in_=in_)
        h.then_inc(*self.bump(sem))
        return h

    def barrier(self):
        targets = {}
        for e in ("tensor", "gpsimd", "vector", "scalar"):
            targets[e] = self.sig(e)
        for e in ENGINES:
            for snm in ("dS", "dG", "dC"):
                self.wait(e, snm)
            for e2, n in targets.items():
                if n > 0:
                    self.q[e].wait_ge(self.sems[ESEM[e2]], n)

    def finalize(self):
        with self.nc.Block() as block:
            block.sync(lambda eng: self.q["sync"].replay(eng))
            block.gpsimd(lambda eng: self.q["gpsimd"].replay(eng))
            block.tensor(lambda eng: self.q["tensor"].replay(eng))
            block.vector(lambda eng: self.q["vector"].replay(eng))
            block.scalar(lambda eng: self.q["scalar"].replay(eng))


def _chvec(v):
    v = np.asarray(v, np.float32)
    n = v.shape[0] // 128
    return np.ascontiguousarray(v.reshape(n, 128).T)


def host_prep(inputs):
    sh = {}
    f32 = lambda a: np.ascontiguousarray(np.asarray(a, np.float32))
    sh["ident"] = np.eye(128, dtype=np.float32)
    sh["identb"] = np.eye(128, dtype=np.float32).astype(BF)
    sh["onescol"] = np.ones((128, 1), np.float32)
    sh["onesrow"] = np.ones((1, 128), np.float32)
    sh["iota8"] = np.broadcast_to(np.arange(8, dtype=np.float32), (128, 8)).copy()
    s_idx = np.arange(128)
    sh["causal"] = (s_idx[:, None] <= s_idx[None, :]).astype(np.float32)
    sh["slt"] = (s_idx[:, None] < s_idx[None, :]).astype(np.float32)
    hm = np.zeros((8, 128, 16), np.float32)
    for j in range(8):
        for h in (2 * j, 2 * j + 1):
            hm[j, np.arange((h % 2) * 64, (h % 2) * 64 + 64), h] = 1.0
    sh["headmask"] = np.ascontiguousarray(hm.transpose(1, 0, 2).reshape(128, 8 * 16))
    exm = np.zeros((8, 16, 128), np.float32)
    for j in range(8):
        exm[j, 2 * j, 0:64] = 1.0
        exm[j, 2 * j + 1, 64:128] = 1.0
    sh["expmask"] = np.ascontiguousarray(exm.transpose(1, 0, 2).reshape(16, 8 * 128))

    sh["ln1g"] = _chvec(inputs["L0_attn_ln1_g"]); sh["ln1b"] = _chvec(inputs["L0_attn_ln1_b"])
    sh["ln2g"] = _chvec(inputs["L0_attn_ln2_g"]); sh["ln2b"] = _chvec(inputs["L0_attn_ln2_b"])
    sh["ln3g"] = _chvec(inputs["L0_attn_ln3_g"]); sh["ln3b"] = _chvec(inputs["L0_attn_ln3_b"])
    sh["dwb"] = _chvec(inputs["L0_attn_dw_b"]); sh["pwb"] = _chvec(inputs["L0_attn_pw_b"])
    sh["fb"] = _chvec(inputs["L0_attn_f_b"])
    sh["mlpb1"] = _chvec(inputs["L0_attn_mlp_b1"]); sh["mlpb2"] = _chvec(inputs["L0_attn_mlp_b2"])
    dw = np.asarray(inputs["L0_attn_dw_w"], np.float32)[:, 0, :]
    sh["dw"] = np.ascontiguousarray(dw.reshape(8, 128, 4).transpose(1, 0, 2).reshape(128, 32))
    sh["clipx"] = f32(inputs["L0_ffn_clip_x"]).reshape(1, 1)
    sh["clipd"] = f32(inputs["L0_ffn_clip_down"]).reshape(1, 1)

    tr = lambda a: np.ascontiguousarray(np.asarray(a, np.float32).T)
    sh["pwT"] = tr(inputs["L0_attn_pw_w"])
    sh["fwT"] = tr(inputs["L0_attn_f_w"])
    sh["dpT"] = tr(inputs["L0_attn_dp_w"])
    sh["m1T"] = tr(inputs["L0_attn_mlp_w1"])
    sh["m2T"] = tr(inputs["L0_attn_mlp_w2"])
    sh["wg"] = f32(inputs["L0_ffn_w_gate"])
    sh["wu"] = f32(inputs["L0_ffn_w_up"])
    sh["wd"] = f32(inputs["L0_ffn_w_down"])
    sh["l1wT"] = tr(inputs["L1_attn_w"])
    sh["l1wo"] = f32(inputs["L1_attn_w_out"])
    sh["routerT"] = tr(inputs["L1_ffn_router_weight"])
    def tile_lhsT(w):
        w = np.asarray(w, np.float32)
        E, O, I = w.shape
        a = w.reshape(E, O // 128, 128, I // 128, 128)
        return np.ascontiguousarray(a.transpose(0, 1, 4, 3, 2))
    sh["preT"] = tile_lhsT(inputs["L1_ffn_ff_pre_act_weight"])
    sh["gateT"] = tile_lhsT(inputs["L1_ffn_gate_weight"])
    sh["postT"] = tile_lhsT(inputs["L1_ffn_ff_post_act_weight"])

    x = np.asarray(inputs["L0_attn_x"], np.float32).reshape(T, D)
    per_core = []
    for i in range(NCORES):
        d = {}
        lo = i * TC
        xs = np.zeros((TC + 3, D), np.float32)
        xs[3:] = x[lo:lo + TC]
        if i > 0:
            xs[:3] = x[lo - 3:lo]
        d["xs"] = xs
        d["hmask"] = np.full((128, 1), 0.0 if i == 0 else 1.0, np.float32)
        rm = np.zeros((128, NCORES), np.float32)
        rm[:, :i] = 1.0
        d["rankmask"] = rm
        rh = np.zeros((128, NCORES), np.float32)
        rh[:, i] = 1.0
        d["rankhot"] = rh
        per_core.append(d)
    return sh, per_core


INPUT_SHAPES = {
    "xs": ([TC + 3, D], F32), "hmask": ([128, 1], F32),
    "ident": ([128, 128], F32), "onescol": ([128, 1], F32), "onesrow": ([1, 128], F32),
    "iota8": ([128, 8], F32), "causal": ([128, 128], F32), "slt": ([128, 128], F32),
    "headmask": ([128, 8 * 16], F32), "expmask": ([16, 8 * 128], F32),
    "ln1g": ([128, 8], F32), "ln1b": ([128, 8], F32),
    "ln2g": ([128, 8], F32), "ln2b": ([128, 8], F32),
    "ln3g": ([128, 8], F32), "ln3b": ([128, 8], F32),
    "dwb": ([128, 8], F32), "pwb": ([128, 8], F32), "fb": ([128, 24], F32),
    "mlpb1": ([128, 32], F32), "mlpb2": ([128, 8], F32), "dw": ([128, 32], F32),
    "clipx": ([1, 1], F32), "clipd": ([1, 1], F32),
    "pwT": ([D, D], F32), "fwT": ([D, 3072], F32), "dpT": ([EXP, D], F32),
    "m1T": ([D, 4096], F32), "m2T": ([4096, D], F32),
    "wg": ([I0, D], F32), "wu": ([I0, D], F32), "wd": ([D, I0], F32),
    "l1wT": ([D, 3072], F32), "l1wo": ([D, D], F32), "routerT": ([D, NEXP], F32),
    "preT": ([NEXP, MLPT, 128, DT, 128], F32), "gateT": ([NEXP, MLPT, 128, DT, 128], F32),
    "postT": ([NEXP, DT, 128, MLPT, 128], F32),
}

SCRATCH = {
    "xn_s": ([D, TC], F32),
    "r1_s": ([1, TC], F32),
    "h1halo_s": ([D, 4], F32),
    "gmaxx_s": ([1, 1], F32),
}


def ln_rows(p, row, sum_ps, sumsq_ps, want_r1=False):
    v, s = p.q["vector"], p.q["scalar"]
    r = row
    inv_d = 1.0 / D
    v.tensor_single_scalar(r(0), sum_ps, inv_d, ALU.mult)
    v.tensor_single_scalar(r(1), sumsq_ps, inv_d, ALU.mult)
    p.vsync()
    v.tensor_tensor(r(2), r(0), r(0), ALU.mult)
    p.vsync()
    v.tensor_tensor(r(3), r(1), r(2), ALU.subtract)
    p.edge("vector", "scalar")
    s.activation(r(4), r(3), AF.Ln, bias=EPS_LN)
    s.activation(r(4), r(4), AF.Exp, scale=-0.5)
    p.edge("scalar", "vector")
    v.scalar_tensor_tensor(r(5), r(0), -1.0, r(4), ALU.mult, ALU.mult)
    if want_r1:
        v.tensor_single_scalar(r(6), r(3), EPS_LN, ALU.add)
        p.vsync()
        v.reciprocal(r(7), r(6))
        p.vsync()
        v.tensor_tensor(r(6), r(3), r(7), ALU.mult)
        p.edge("vector", "scalar")
        s.activation(r(6), r(6), AF.Ln, bias=EPS_LN)
        s.activation(r(6), r(6), AF.Exp, scale=-0.5)
        p.edge("scalar", "vector")


_LC_SEQ = [0]


def load_consts(p, io, names, st):
    _LC_SEQ[0] += 1
    for nm in names:
        shape, dt = INPUT_SHAPES[nm]
        tile = p.sb(f"{nm}_sb{_LC_SEQ[0]}", shape, dt, st)
        p.dma("sync", tile[:, :], io[nm][:, :] if len(shape) == 2 else io[nm])
        io[nm + "_sb"] = tile
    for e in ENGINES:
        p.wait(e, "dS")


def emit_F0(p: Prog, io: dict, st: contextlib.ExitStack):
    nc = p.nc
    xs = io["xs"]
    xt = p.sb("f0_xt", [128, D], F32, st)
    xcl = p.sb("f0_xcl", [128, DT * 128], F32, st)
    xsq = p.sb("f0_xsq", [128, DT * 128], F32, st)
    rowbuf = p.sb("f0_rows", [1, 128 * 8], F32, st)
    xnblk = p.sb("f0_xn", [128, DT * 128], F32, st)
    mx = p.sb("f0_mx", [128, 16], F32, st)
    tpz = p.sb("f0_tpz", [128, 128], F32, st)
    gmax = p.sb("f0_gmax", [1, 1], F32, st)
    ident = io["ident_sb"]
    onescol = io["onescol_sb"]
    onesrow = io["onesrow_sb"]

    cc_in = nc.dram_tensor("f0cc_in", [1, 1], F32)
    cc_out = nc.dram_tensor("f0cc_out", [1, 1], F32)

    v, s, t, g, sy = (p.q[e] for e in ("vector", "scalar", "tensor", "gpsimd", "sync"))

    g.memset(mx[:], 0.0)
    g.memset(tpz[:], 0.0)
    p.sig("gpsimd")
    for e in ("sync", "tensor", "vector"):
        p.wait(e, "gE")

    def tile(tt, W, halo):
        row = lambda i: rowbuf[0:1, 128 * i:128 * i + W]
        src_row = 0 if halo else 3 + tt * 128
        p.dma("sync", xt[0:W, :], xs[src_row:src_row + W, :])
        p.wait("tensor", "dS")
        for c in range(DT):
            bank = c % 2
            if c >= 2:
                p.wait("tensor", "vE")
            t.transpose(p.ps(bank)[:, 0:128], xt[:, c * 128:(c + 1) * 128], ident[:])
            p.sig("tensor")
            p.wait("vector", "tE")
            v.tensor_copy(xcl[:, c * 128:c * 128 + W], p.ps(bank)[:, 0:W])
            p.sig("vector")
        p.edge("vector", "scalar")
        for c in range(DT):
            s.activation(xsq[:, c * 128:c * 128 + W], xcl[:, c * 128:c * 128 + W], AF.Square)
        p.edge("scalar", "tensor")
        for c in range(DT):
            t.matmul(p.ps(2)[0:1, 0:W], onescol[:], xcl[:, c * 128:c * 128 + W],
                     start=(c == 0), stop=(c == DT - 1))
        for c in range(DT):
            t.matmul(p.ps(5)[0:1, 0:W], onescol[:], xsq[:, c * 128:c * 128 + W],
                     start=(c == 0), stop=(c == DT - 1))
        p.edge("tensor", "vector")
        ln_rows(p, row, p.ps(2)[0:1, 0:W], p.ps(5)[0:1, 0:W], want_r1=True)
        if not halo:
            p.sig("vector")
            p.wait("sync", "vE")
            p.dma("sync", io["r1_s"][0:1, tt * 128:tt * 128 + W], row(6))
        p.edge("vector", "tensor")
        t.matmul(p.ps(3)[:, 0:W], onesrow[:], row(4), start=True, stop=True)
        t.matmul(p.ps(4)[:, 0:W], onesrow[:], row(5), start=True, stop=True)
        if halo:
            t.matmul(p.ps(6)[:, 0:W], onesrow[:], row(6), start=True, stop=True)
        p.edge("tensor", "vector")
        for c in range(DT):
            sl = slice(c * 128, c * 128 + W)
            v.tensor_tensor(xnblk[:, sl], xcl[:, sl], p.ps(3)[:, 0:W], ALU.mult)
        p.vsync()
        for c in range(DT):
            sl = slice(c * 128, c * 128 + W)
            v.tensor_tensor(xnblk[:, sl], xnblk[:, sl], p.ps(4)[:, 0:W], ALU.add)
        p.vsync()
        if not halo:
            for c in range(DT):
                v.reduce_max(mx[:, c + 1:c + 2], xnblk[:, c * 128:(c + 1) * 128],
                             axis=AX.X, apply_absolute_value=True)
            p.vsync()
            v.reduce_max(mx[:, 9:10], mx[:, 1:9], axis=AX.X)
            p.vsync()
            v.tensor_tensor(mx[:, 0:1], mx[:, 0:1], mx[:, 9:10], ALU.max)
            p.sig("vector")
            p.wait("sync", "vE")
            for c in range(DT):
                p.dma("sync", io["xn_s"][c * 128:(c + 1) * 128, tt * 128:(tt + 1) * 128],
                      xnblk[:, c * 128:(c + 1) * 128])
            p.wait("vector", "dS")
        else:
            for c in range(DT):
                sl = slice(c * 128, c * 128 + W)
                v.tensor_tensor(xnblk[:, sl], xnblk[:, sl], p.ps(6)[:, 0:W], ALU.mult)
                p.vsync()
                v.tensor_scalar(xnblk[:, sl], xnblk[:, sl],
                                io["ln1g_sb"][:, c:c + 1], io["ln1b_sb"][:, c:c + 1],
                                ALU.mult, ALU.add)
                p.vsync()
                v.tensor_single_scalar(xnblk[:, sl], xnblk[:, sl], io["hmask_sb"][:, 0:1], ALU.mult)
            p.sig("vector")
            p.wait("sync", "vE")
            for c in range(DT):
                p.dma("sync", io["h1halo_s"][c * 128:(c + 1) * 128, 0:W], xnblk[:, c * 128:c * 128 + W])
            p.wait("vector", "dS")

    for tt in range(NTT):
        tile(tt, 128, False)
    tile(None, 3, True)

    v.tensor_copy(tpz[:, 0:1], mx[:, 0:1])
    p.edge("vector", "tensor")
    t.transpose(p.ps(0)[:, 0:128], tpz[:], ident[:])
    p.edge("tensor", "vector")
    v.reduce_max(gmax[0:1, 0:1], p.ps(0)[0:1, 0:128], axis=AX.X)
    p.sig("vector")
    p.wait("sync", "vE")
    p.dma("sync", cc_in.ap(), gmax[0:1, 0:1])
    p.wait("gpsimd", "dS")
    p.q["gpsimd"].rec("collective_compute", "AllReduce", ALU.max, replica_groups=RG,
                      ins=[cc_in.ap().opt()], outs=[cc_out.ap().opt()]).then_inc(*p.bump("cc", 1))
    p.wait("sync", "cc")
    p.dma("sync", gmax[0:1, 0:1], cc_out.ap())
    p.wait("sync", "dS")
    p.dma("sync", io["gmaxx_s"][0:1, 0:1], gmax[0:1, 0:1])


PHASES = {
    "F0": dict(
        emit=emit_F0,
        inputs=["xs"],
        consts=["ident", "onescol", "onesrow", "ln1g", "ln1b", "hmask"],
        scratch_r=[],
        scratch_w=["xn_s", "r1_s", "h1halo_s", "gmaxx_s"],
    ),
}


def build_program(phase_names, fused_scratch):
    nc = bass.Bass("TRN2", target_bir_lowering=False)
    p = Prog(nc)
    io = {}
    need_in, need_const, scr_r, scr_w = set(), set(), set(), set()
    for ph in phase_names:
        cfg = PHASES[ph]
        need_in.update(cfg["inputs"])
        need_const.update(cfg["consts"])
        scr_r.update(cfg["scratch_r"])
        scr_w.update(cfg["scratch_w"])
    for nm in sorted(need_in | need_const):
        shape, dt = INPUT_SHAPES[nm]
        io[nm] = nc.dram_tensor(nm, shape, dt, kind="ExternalInput").ap()
    for nm in sorted(scr_r | scr_w):
        shape, dt = SCRATCH[nm]
        if nm in fused_scratch:
            kind = "Internal"
        elif nm in scr_w:
            kind = "ExternalOutput"
        else:
            kind = "ExternalInput"
        io[nm] = nc.dram_tensor(nm, shape, dt, kind=kind).ap()
    CONVERT = [
        ("pwT", [D, D]), ("fwT", [D, 3072]),
        ("dpT", [EXP, D]), ("m1T", [D, 4096]), ("m2T", [4096, D]),
        ("l1wT", [D, 3072]), ("l1wo", [D, D]),
    ]
    io["_dpthr"] = {}
    conv_needed = [c for c in CONVERT if c[0] in need_in]
    moe_needed = ("preT" in need_in)
    for nm, shape in conv_needed:
        bt = nc.dram_tensor(nm + "_b", shape, BF16)
        io[nm + "_b"] = bt.ap()
        p.q["gpsimd"].rec("dma_start", out=bt.ap(), in_=io[nm]).then_inc(*p.bump("dP"))
        io["_dpthr"][nm] = p.val("dP")
    if moe_needed:
        for nm in ("preT", "gateT", "postT"):
            shape = INPUT_SHAPES[nm][0]
            bt = nc.dram_tensor(nm + "_b", shape, BF16)
            io[nm + "_b"] = bt.ap()
            for e in range(NEXP):
                p.q["gpsimd"].rec("dma_start", out=bt.ap()[e], in_=io[nm][e]).then_inc(*p.bump("dP"))
            io["_dpthr"][nm] = p.val("dP")
    CONSTVALS = [EPS_LN, MAGIC, -MAGIC]
    ctile = p.sb("cvals", [128, len(CONSTVALS)], F32)
    for ci, cv in enumerate(CONSTVALS):
        p.q["gpsimd"].memset(ctile[:, ci:ci + 1], cv)
        nc.const_aps.register(F32, cv, ctile[:, ci:ci + 1])
    p.sig("gpsimd")
    for e in ENGINES:
        p.wait(e, "gE")
    for ph in phase_names:
        with contextlib.ExitStack() as st:
            load_consts(p, io, PHASES[ph]["consts"], st)
            PHASES[ph]["emit"](p, io, st)
            p.barrier()
    for e in ENGINES:
        for dsem in ("dS", "dG", "dC"):
            pass
    p.wait("gpsimd", "dS")
    p.wait("gpsimd", "dG")
    p.wait("gpsimd", "dC")
    p.finalize()
    return nc, sorted(need_in | need_const), sorted(scr_r - set(fused_scratch)), sorted(scr_w - set(fused_scratch))


def run_phases(launch_groups, inputs, want=None, store0=None, fused=None):
    sh, per_core = host_prep(inputs)
    store = store0 if store0 is not None else [dict() for _ in range(NCORES)]
    n_l = 0
    for group in launch_groups:
        nc, ext_in, scr_in, scr_out = build_program(group, fused_scratch=fused or set())
        in_maps = []
        for i in range(NCORES):
            m = {}
            for nm in ext_in:
                m[nm] = per_core[i][nm] if nm in per_core[i] else sh[nm]
            for nm in scr_in:
                m[nm] = store[i][nm]
            in_maps.append(m)
        res = run_bass_kernel_spmd(nc, in_maps, core_ids=list(range(NCORES)))
        n_l += 1
        for i in range(NCORES):
            for nm in scr_out:
                store[i][nm] = np.asarray(res.results[i][nm])
    return store


ALL_PHASES = [["F0"], ["F1"], ["F2"], ["F3"], ["F4a"], ["F4b"], ["F4c"],
              ["F5"], ["F6a"], ["F6b"]]


FUSED = True


def kernel(**inputs):
    if FUSED:
        store = run_phases([[g[0] for g in ALL_PHASES]], inputs,
                           fused=set(SCRATCH) - {"y"})
        out = np.concatenate([store[i]["y"] for i in range(NCORES)], axis=0)
        return out.reshape(1, T, D).astype(np.float32)
    store = run_phases(ALL_PHASES, inputs)
    out = np.concatenate([store[i]["y"] for i in range(NCORES)], axis=0)
    return out.reshape(1, T, D).astype(np.float32)


def quant_sweeps(p, pairs, scale_col, iscr, fscr):
    v = p.q["vector"]
    for i, (src, dst) in enumerate(pairs):
        v.tensor_single_scalar(fscr[i], src, scale_col, ALU.mult)
    p.vsync()
    for i, (src, dst) in enumerate(pairs):
        v.tensor_copy(iscr[i], fscr[i])
    p.vsync()
    for i, (src, dst) in enumerate(pairs):
        v.tensor_copy(fscr[i], iscr[i])
    p.vsync()
    for i, (src, dst) in enumerate(pairs):
        v.tensor_single_scalar(fscr[i], fscr[i], QMAX, ALU.min)
    p.vsync()
    for i, (src, dst) in enumerate(pairs):
        v.tensor_single_scalar(fscr[i], fscr[i], -QMAX - 1.0, ALU.max)
    p.vsync()
    for i, (src, dst) in enumerate(pairs):
        v.tensor_copy(dst, fscr[i])
    p.vsync()


def bcast_scalar_col(p, dst_col, src11, bank=6):
    p.edge("vector", "tensor")
    p.q["tensor"].matmul(p.ps(bank)[:, 0:1], p.nc_onesrow[:], src11, start=True, stop=True)
    p.edge("tensor", "vector")
    p.q["vector"].tensor_copy(dst_col, p.ps(bank)[:, 0:1])
    p.vsync()


def emit_F1(p: Prog, io: dict, st: contextlib.ExitStack):
    nc = p.nc
    v, s, t, g, sy = (p.q[e] for e in ("vector", "scalar", "tensor", "gpsimd", "sync"))
    identb = io["identb_sb"]
    onesrow = io["onesrow_sb"]
    p.nc_onesrow = onesrow

    qgT = p.sb("f1_qgT", [128, DT * I0], BF16, st)
    quT = p.sb("f1_quT", [128, DT * I0], BF16, st)
    sg = p.sb("f1_sg", [128, 2 * I0T + 4], F32, st)
    sxg = p.sb("f1_sxg", [128, 2 * I0T], F32, st)
    wnat = p.sb("f1_wnat", [128, D], F32, st)
    wfs = p.sb("f1_wfs", [128, D], F32, st)
    wis = p.sb("f1_wis", [128, D], mybir.dt.int32, st)
    qbf = p.sb("f1_qbf", [128, D], BF16, st)
    scl = p.sb("f1_scl", [128, 4], F32, st)
    xch = p.sb("f1_xch", [128, DT * 512], F32, st)
    xfs = p.sb("f1_xfs", [128, DT * 512], F32, st)
    xis = p.sb("f1_xis", [128, DT * 512], mybir.dt.int32, st)
    qx = p.sb("f1_qx", [128, DT * 512], BF16, st)
    st1 = [p.sb(f"f1_st1{i}", [128, 512], F32, st) for i in range(2)]
    st2 = [p.sb(f"f1_st2{i}", [128, 512], F32, st) for i in range(2)]
    hmt2 = [p.sb(f"f1_hmt{i}", [128, 512], F32, st) for i in range(2)]
    f1vs, f1ss, f1st, f1ctr = [], [], [], [0]
    rows = p.sb("f1_rows", [1, 8], F32, st)
    cols = p.sb("f1_cols", [128, 8], F32, st)
    hmx = p.sb("f1_hmx", [128, 4], F32, st)
    tpz = p.sb("f1_tpz", [128, 128], F32, st)
    gm = p.sb("f1_gm", [1, 1], F32, st)
    ident = io["ident_sb"]

    cc_in = nc.dram_tensor("f1cc_in", [1, 1], F32)
    cc_out = nc.dram_tensor("f1cc_out", [1, 1], F32)

    g.memset(hmx[:], 0.0)
    g.memset(tpz[:], 0.0)
    p.sig("gpsimd")
    for e in ("sync", "tensor", "vector", "scalar"):
        p.wait(e, "gE")

    p.dma("sync", rows[0:1, 0:1], io["clipx"][0:1, 0:1])
    p.dma("sync", rows[0:1, 1:2], io["gmaxx_s"][0:1, 0:1])
    p.wait("scalar", "dS")
    s.activation(rows[0:1, 2:3], rows[0:1, 0:1], AF.Sigmoid)
    p.edge("scalar", "vector")
    v.tensor_tensor(rows[0:1, 3:4], rows[0:1, 2:3], rows[0:1, 1:2], ALU.mult)
    p.vsync()
    v.reciprocal(rows[0:1, 4:5], rows[0:1, 3:4])
    p.vsync()
    v.tensor_single_scalar(rows[0:1, 5:6], rows[0:1, 4:5], QMAX, ALU.mult)
    v.tensor_single_scalar(rows[0:1, 6:7], rows[0:1, 3:4], 1.0 / QMAX, ALU.mult)
    p.vsync()
    bcast_scalar_col(p, cols[:, 0:1], rows[0:1, 5:6])
    bcast_scalar_col(p, cols[:, 1:2], rows[0:1, 6:7])

    def weight_prep(w_ext, qT, sgcol0):
        for it in range(I0T):
            p.wait("sync", "vE")
            p.dma("sync", wnat[:], w_ext[it * 128:(it + 1) * 128, :])
            p.wait("vector", "dS")
            v.reduce_max(scl[:, 0:1], wnat[:], axis=AX.X, apply_absolute_value=True)
            p.vsync()
            v.reciprocal(scl[:, 1:2], scl[:, 0:1])
            p.vsync()
            v.tensor_single_scalar(scl[:, 2:3], scl[:, 1:2], QMAX, ALU.mult)
            v.tensor_single_scalar(sg[:, sgcol0 + it:sgcol0 + it + 1], scl[:, 0:1], 1.0 / QMAX, ALU.mult)
            p.vsync()
            quant_sweeps(p, [(wnat[:], qbf[:])], scl[:, 2:3], [wis[:]], [wfs[:]])
            p.edge("vector", "tensor")
            for ht in range(DT):
                t.transpose(p.psbf[:, 0:128], qbf[:, ht * 128:(ht + 1) * 128], identb[:])
                p.edge("tensor", "vector")
                v.tensor_copy(qT[:, ht * I0 + it * 128:ht * I0 + (it + 1) * 128], p.psbf[:, 0:128])
                p.edge("vector", "tensor")
            p.edge("tensor", "vector")
            p.sig("vector")

    weight_prep(io["wg"], qgT, 0)
    weight_prep(io["wu"], quT, I0T)
    v.tensor_single_scalar(sxg[:, 0:2 * I0T], sg[:, 0:2 * I0T], cols[:, 1:2], ALU.mult)
    p.vsync()

    for nb in range(4):
        p.wait("sync", "vE")
        for ct in range(DT):
            p.dma("sync", xch[:, ct * 512:(ct + 1) * 512],
                  io["xn_s"][ct * 128:(ct + 1) * 128, nb * 512:(nb + 1) * 512])
        p.wait("vector", "dS")
        quant_sweeps(p, [(xch[:, ct * 512:(ct + 1) * 512], qx[:, ct * 512:(ct + 1) * 512]) for ct in range(DT)],
                     cols[:, 0:1],
                     [xis[:, ct * 512:(ct + 1) * 512] for ct in range(DT)],
                     [xfs[:, ct * 512:(ct + 1) * 512] for ct in range(DT)])
        p.edge("vector", "tensor")
        for it in range(I0T):
            j = f1ctr[0]; f1ctr[0] += 1
            bg = 0 if it % 2 == 0 else 1
            bu = 2 if it % 2 == 0 else 3
            if j >= 2:
                p.wait("tensor", "vE", f1vs[j - 2])
                p.wait("tensor", "sE", f1ss[j - 2])
            for ct in range(DT):
                t.matmul(p.ps(bg)[:, 0:512], qgT[:, ct * I0 + it * 128:ct * I0 + (it + 1) * 128],
                         qx[:, ct * 512:(ct + 1) * 512], start=(ct == 0), stop=(ct == DT - 1))
            for ct in range(DT):
                t.matmul(p.ps(bu)[:, 0:512], quT[:, ct * I0 + it * 128:ct * I0 + (it + 1) * 128],
                         qx[:, ct * 512:(ct + 1) * 512], start=(ct == 0), stop=(ct == DT - 1))
            p.edge("tensor", "scalar")
            s.activation(st1[j % 2][:], p.ps(bg)[:, 0:512], AF.Silu, scale=sxg[:, it:it + 1])
            s.activation(st2[j % 2][:], p.ps(bu)[:, 0:512], AF.Identity, scale=sxg[:, I0T + it:I0T + it + 1])
            f1ss.append(p.sig("scalar"))
            p.wait("vector", "sE", f1ss[j])
            if j >= 2:
                p.wait("vector", "dS", f1st[j - 2])
            v.tensor_tensor(hmt2[j % 2][:], st1[j % 2][:], st2[j % 2][:], ALU.mult)
            p.vsync()
            v.reduce_max(hmx[:, 1:2], hmt2[j % 2][:], axis=AX.X, apply_absolute_value=True)
            p.vsync()
            v.tensor_tensor(hmx[:, 0:1], hmx[:, 0:1], hmx[:, 1:2], ALU.max)
            f1vs.append(p.sig("vector"))
            p.wait("sync", "vE", f1vs[j])
            p.dma("sync", io["hmid_s"][it * 128:(it + 1) * 128, nb * 512:(nb + 1) * 512], hmt2[j % 2][:])
            f1st.append(p.val("dS"))

    v.tensor_copy(tpz[:, 0:1], hmx[:, 0:1])
    p.edge("vector", "tensor")
    t.transpose(p.ps(0)[:, 0:128], tpz[:], ident[:])
    p.edge("tensor", "vector")
    v.reduce_max(gm[0:1, 0:1], p.ps(0)[0:1, 0:128], axis=AX.X)
    p.sig("vector")
    p.wait("sync", "vE")
    p.dma("sync", cc_in.ap(), gm[0:1, 0:1])
    p.wait("gpsimd", "dS")
    p.q["gpsimd"].rec("collective_compute", "AllReduce", ALU.max, replica_groups=RG,
                      ins=[cc_in.ap().opt()], outs=[cc_out.ap().opt()]).then_inc(*p.bump("cc", 1))
    p.wait("sync", "cc")
    p.dma("sync", gm[0:1, 0:1], cc_out.ap())
    p.wait("sync", "dS")
    p.dma("sync", io["gmaxh_s"][0:1, 0:1], gm[0:1, 0:1])


PHASES["F1"] = dict(
    emit=emit_F1,
    inputs=["wg", "wu", "clipx"],
    consts=["ident", "identb", "onesrow"],
    scratch_r=["xn_s", "gmaxx_s"],
    scratch_w=["hmid_s", "gmaxh_s"],
)
SCRATCH["hmid_s"] = ([I0, TC], F32)
SCRATCH["gmaxh_s"] = ([1, 1], F32)
INPUT_SHAPES["identb"] = ([128, 128], BF16)


def emit_F2(p: Prog, io: dict, st: contextlib.ExitStack):
    nc = p.nc
    v, s, t, g, sy = (p.q[e] for e in ("vector", "scalar", "tensor", "gpsimd", "sync"))
    ident = io["ident_sb"]
    onescol = io["onescol_sb"]
    onesrow = io["onesrow_sb"]
    NB = 512

    pwT = p.sb("f2_pwT", [128, DT * D], BF16, st)
    fwT = p.sb("f2_fwT", [128, DT * 3072], BF16, st)
    h1b = p.sb("f2_h1", [128, DT * (NB + 3)], F32, st)
    xnc = p.sb("f2_xnc", [128, DT * NB], F32, st)
    cbf = p.sb("f2_cbf", [128, DT * NB], BF16, st)
    sq = p.sb("f2_sq", [128, DT * NB], F32, st)
    g2b = p.sb("f2_g2b", [128, DT * NB], BF16, st)
    coeff = p.sb("f2_coeff", [128, EXPT * NB], F32, st)
    zu = p.sb("f2_zu", [128, EXPT * NB], F32, st)
    hxw = p.sb("f2_hxw", [128, NB * 3], F32, st)
    aout = p.sb("f2_aout", [128, NB], F32, st)
    bout = p.sb("f2_bout", [128, NB], F32, st)
    ast = p.sb("f2_ast", [128, EXPT], F32, st)
    bst = p.sb("f2_bst", [128, EXPT], F32, st)
    zer = p.sb("f2_zer", [128, NB], F32, st)
    rowbuf = p.sb("f2_rows", [1, 512 * 6], F32, st)
    nfb = p.sb("f2_nfb", [128, 24], F32, st)
    st1 = p.sb("f2_st1", [128, NB], F32, st)
    hin = p.sb("f2_hin", [128, EXPT], F32, st)
    gat = p.sb("f2_gat", [128, 2 * EXPT], F32, st)

    cc_in = nc.dram_tensor("f2cc_in", [128, 2 * EXPT], F32)
    cc_out = nc.dram_tensor("f2cc_out", [NCORES * 128, 2 * EXPT], F32)

    p.wait("sync", "dP", io["_dpthr"]["fwT"])
    for ct in range(DT):
        p.dma("sync", pwT[:, ct * D:(ct + 1) * D], io["pwT_b"][ct * 128:(ct + 1) * 128, :])
        p.dma("sync", fwT[:, ct * 3072:(ct + 1) * 3072], io["fwT_b"][ct * 128:(ct + 1) * 128, :])
    g.memset(ast[:], 1.0)
    g.memset(bst[:], 0.0)
    g.memset(zer[:], 0.0)
    p.sig("gpsimd")
    for e in ("sync", "tensor", "vector", "scalar"):
        p.wait(e, "dG")
        p.wait(e, "gE")
    v.tensor_single_scalar(nfb[:, 0:24], io["fb_sb"][:, 0:24], -1.0, ALU.mult)
    p.vsync()
    p.edge("vector", "scalar")
    p.wait("sync", "vE")
    for ct in range(DT):
        p.dma("sync", h1b[:, ct * (NB + 3):ct * (NB + 3) + 3], io["h1halo_s"][ct * 128:(ct + 1) * 128, 0:3])

    row = lambda i: rowbuf[0:1, 512 * i:512 * i + NB]

    for nb in range(4):
        p.wait("sync", "vE")
        p.wait("sync", "dS")
        for ct in range(DT):
            p.dma("sync", xnc[:, ct * NB:(ct + 1) * NB],
                  io["xn_s"][ct * 128:(ct + 1) * 128, nb * NB:(nb + 1) * NB])
        p.dma("sync", rowbuf[0:1, 0:NB], io["r1_s"][0:1, nb * NB:(nb + 1) * NB])
        p.wait("tensor", "dS")
        t.matmul(p.ps(3)[:, 0:NB], onesrow[:], rowbuf[0:1, 0:NB], start=True, stop=True)
        p.edge("tensor", "vector")
        p.wait("vector", "dS")
        for ct in range(DT):
            v.tensor_tensor(h1b[:, ct * (NB + 3) + 3:ct * (NB + 3) + 3 + NB],
                            xnc[:, ct * NB:(ct + 1) * NB], p.ps(3)[:, 0:NB], ALU.mult)
        p.vsync()
        for ct in range(DT):
            v.tensor_scalar(h1b[:, ct * (NB + 3) + 3:ct * (NB + 3) + 3 + NB],
                            h1b[:, ct * (NB + 3) + 3:ct * (NB + 3) + 3 + NB],
                            io["ln1g_sb"][:, ct:ct + 1], io["ln1b_sb"][:, ct:ct + 1],
                            ALU.mult, ALU.add)
        p.vsync()
        for ct in range(DT):
            b0 = ct * (NB + 3)
            v.tensor_scalar(sq[:, ct * NB:(ct + 1) * NB], h1b[:, b0:b0 + NB],
                            io["dw_sb"][:, ct * 4:ct * 4 + 1], None, ALU.mult)
        p.vsync()
        for k in range(1, 4):
            for ct in range(DT):
                b0 = ct * (NB + 3)
                v.scalar_tensor_tensor(sq[:, ct * NB:(ct + 1) * NB], h1b[:, b0 + k:b0 + k + NB],
                                       io["dw_sb"][:, ct * 4 + k:ct * 4 + k + 1],
                                       sq[:, ct * NB:(ct + 1) * NB], ALU.mult, ALU.add)
            p.vsync()
        for ct in range(DT):
            v.tensor_scalar(sq[:, ct * NB:(ct + 1) * NB], sq[:, ct * NB:(ct + 1) * NB],
                            io["dwb_sb"][:, ct:ct + 1], None, ALU.add)
        p.vsync()
        for ct in range(DT):
            v.tensor_copy(cbf[:, ct * NB:(ct + 1) * NB], sq[:, ct * NB:(ct + 1) * NB])
        p.vsync()
        for ct in range(DT):
            b0 = ct * (NB + 3)
            v.tensor_copy(h1b[:, b0:b0 + 3], h1b[:, b0 + NB:b0 + NB + 3])
        p.vsync()
        for ot in range(DT):
            bank = ot % 2
            p.edge("vector", "tensor")
            for ct in range(DT):
                t.matmul(p.ps(bank)[:, 0:NB], pwT[:, ct * D + ot * 128:ct * D + (ot + 1) * 128],
                         cbf[:, ct * NB:(ct + 1) * NB], start=(ct == 0), stop=(ct == DT - 1))
            p.edge("tensor", "scalar")
            s.activation(st1[:], p.ps(bank)[:, 0:NB], AF.Identity, bias=io["pwb_sb"][:, ot:ot + 1])
            p.edge("scalar", "vector")
            v.tensor_tensor(xnc[:, ot * NB:(ot + 1) * NB], st1[:], xnc[:, ot * NB:(ot + 1) * NB], ALU.add)
            p.vsync()
        p.sig("vector")
        p.wait("sync", "vE")
        for ot in range(DT):
            p.dma("sync", io["x1_s"][ot * 128:(ot + 1) * 128, nb * NB:(nb + 1) * NB],
                  xnc[:, ot * NB:(ot + 1) * NB])
        p.edge("vector", "scalar")
        for ct in range(DT):
            s.activation(sq[:, ct * NB:(ct + 1) * NB], xnc[:, ct * NB:(ct + 1) * NB], AF.Square)
        p.edge("scalar", "tensor")
        for ct in range(DT):
            t.matmul(p.ps(2)[0:1, 0:NB], onescol[:], xnc[:, ct * NB:(ct + 1) * NB],
                     start=(ct == 0), stop=(ct == DT - 1))
        for ct in range(DT):
            t.matmul(p.ps(5)[0:1, 0:NB], onescol[:], sq[:, ct * NB:(ct + 1) * NB],
                     start=(ct == 0), stop=(ct == DT - 1))
        p.edge("tensor", "vector")
        ln_rows(p, row, p.ps(2)[0:1, 0:NB], p.ps(5)[0:1, 0:NB], want_r1=False)
        p.edge("vector", "tensor")
        t.matmul(p.ps(3)[:, 0:NB], onesrow[:], row(4), start=True, stop=True)
        t.matmul(p.ps(4)[:, 0:NB], onesrow[:], row(5), start=True, stop=True)
        p.edge("tensor", "vector")
        for ct in range(DT):
            v.tensor_tensor(sq[:, ct * NB:(ct + 1) * NB], xnc[:, ct * NB:(ct + 1) * NB],
                            p.ps(3)[:, 0:NB], ALU.mult)
        p.vsync()
        for ct in range(DT):
            v.tensor_tensor(sq[:, ct * NB:(ct + 1) * NB], sq[:, ct * NB:(ct + 1) * NB],
                            p.ps(4)[:, 0:NB], ALU.add)
        p.vsync()
        for ct in range(DT):
            v.tensor_scalar(sq[:, ct * NB:(ct + 1) * NB], sq[:, ct * NB:(ct + 1) * NB],
                            io["ln2g_sb"][:, ct:ct + 1], io["ln2b_sb"][:, ct:ct + 1],
                            ALU.mult, ALU.add)
        p.vsync()
        for ct in range(DT):
            v.tensor_copy(g2b[:, ct * NB:(ct + 1) * NB], sq[:, ct * NB:(ct + 1) * NB])
        p.vsync()
        for jt in range(24):
            bank = jt % 2
            p.edge("vector", "tensor") if jt < 2 else p.wait("tensor", "sE")
            for ct in range(DT):
                t.matmul(p.ps(bank)[:, 0:NB], fwT[:, ct * 3072 + jt * 128:ct * 3072 + (jt + 1) * 128],
                         g2b[:, ct * NB:(ct + 1) * NB], start=(ct == 0), stop=(ct == DT - 1))
            p.edge("tensor", "scalar")
            if jt < EXPT:
                s.activation(coeff[:, jt * NB:(jt + 1) * NB], p.ps(bank)[:, 0:NB],
                             AF.Sigmoid, scale=-1.0, bias=nfb[:, jt:jt + 1])
                s.activation(zu[:, jt * NB:(jt + 1) * NB], p.ps(bank)[:, 0:NB],
                             AF.Sigmoid, bias=io["fb_sb"][:, jt:jt + 1])
                p.sig("scalar")
            else:
                et = jt - EXPT
                p.edge("vector", "scalar")
                s.activation(hxw[:, 0:NB], p.ps(bank)[:, 0:NB], AF.Sigmoid, bias=io["fb_sb"][:, jt:jt + 1])
                s.activation(hxw[:, NB:2 * NB], p.ps(bank)[:, 0:NB], AF.Relu, bias=io["fb_sb"][:, jt:jt + 1])
                p.edge("scalar", "vector")
                v.tensor_tensor(hxw[:, 2 * NB:3 * NB], hxw[:, NB:2 * NB], hxw[:, 0:NB], ALU.subtract)
                p.vsync()
                v.tensor_single_scalar(hxw[:, 2 * NB:3 * NB], hxw[:, 2 * NB:3 * NB], 0.5, ALU.add)
                p.edge("vector", "scalar")
                s.activation(hxw[:, NB:2 * NB], hxw[:, NB:2 * NB], AF.Sign)
                p.edge("scalar", "vector")
                v.tensor_tensor(hxw[:, NB:2 * NB], hxw[:, NB:2 * NB], hxw[:, 2 * NB:3 * NB], ALU.mult)
                p.vsync()
                v.tensor_tensor(hxw[:, 0:NB], hxw[:, 0:NB], hxw[:, NB:2 * NB], ALU.add)
                p.vsync()
                v.tensor_tensor(zu[:, et * NB:(et + 1) * NB], zu[:, et * NB:(et + 1) * NB],
                                hxw[:, 0:NB], ALU.mult)
                p.vsync()
                p.sig("scalar")
        p.edge("scalar", "vector")
        for et in range(EXPT):
            v.tensor_tensor_scan(aout[:], coeff[:, et * NB:(et + 1) * NB], zer[:],
                                 ast[:, et:et + 1], ALU.mult, ALU.add)
            p.vsync()
            v.tensor_tensor_scan(bout[:], coeff[:, et * NB:(et + 1) * NB],
                                 zu[:, et * NB:(et + 1) * NB], bst[:, et:et + 1], ALU.mult, ALU.add)
            p.vsync()
            v.tensor_copy(ast[:, et:et + 1], aout[:, NB - 1:NB])
            v.tensor_copy(bst[:, et:et + 1], bout[:, NB - 1:NB])
            p.sig("vector")
            p.wait("sync", "vE")
            p.dma("sync", io["a_s"][et * 128:(et + 1) * 128, nb * NB:(nb + 1) * NB], aout[:])
            p.dma("sync", io["b_s"][et * 128:(et + 1) * 128, nb * NB:(nb + 1) * NB], bout[:])
            p.wait("vector", "dS")

    v.tensor_copy(gat[:, 0:EXPT], ast[:, 0:EXPT])
    v.tensor_copy(gat[:, EXPT:2 * EXPT], bst[:, 0:EXPT])
    p.sig("vector")
    p.wait("sync", "vE")
    p.dma("sync", cc_in.ap(), gat[:, :])
    p.wait("gpsimd", "dS")
    p.q["gpsimd"].rec("collective_compute", "AllGather", ALU.bypass, replica_groups=RG,
                      ins=[cc_in.ap().opt()], outs=[cc_out.ap().opt()]).then_inc(*p.bump("cc", 1))
    p.wait("sync", "cc")
    g.memset(hin[:], 0.5)
    p.sig("gpsimd")
    p.wait("vector", "gE")
    for r in range(NCORES - 1):
        p.edge("vector", "sync")
        p.dma("sync", gat[:, :], cc_out[r * 128:(r + 1) * 128, :])
        p.wait("vector", "dS")
        v.tensor_tensor(ast[:, 0:EXPT], gat[:, 0:EXPT], hin[:, 0:EXPT], ALU.mult)
        p.vsync()
        v.tensor_tensor(ast[:, 0:EXPT], ast[:, 0:EXPT], gat[:, EXPT:2 * EXPT], ALU.add)
        p.vsync()
        v.tensor_tensor(ast[:, 0:EXPT], ast[:, 0:EXPT], hin[:, 0:EXPT], ALU.subtract)
        p.vsync()
        v.scalar_tensor_tensor(hin[:, 0:EXPT], ast[:, 0:EXPT], io["rankmask_sb"][:, r:r + 1],
                               hin[:, 0:EXPT], ALU.mult, ALU.add)
        p.vsync()
    p.wait("sync", "vE")
    p.dma("sync", io["hin_s"][:, 0:EXPT], hin[:, 0:EXPT])
    p.wait("sync", "dS")


PHASES["F2"] = dict(
    emit=emit_F2,
    inputs=["pwT", "fwT"],
    consts=["ident", "onescol", "onesrow", "ln1g", "ln1b", "ln2g", "ln2b",
            "dw", "dwb", "pwb", "fb", "rankmask"],
    scratch_r=["xn_s", "r1_s", "h1halo_s"],
    scratch_w=["x1_s", "a_s", "b_s", "hin_s"],
)
SCRATCH["x1_s"] = ([D, TC], F32)
SCRATCH["a_s"] = ([EXP, TC], F32)
SCRATCH["b_s"] = ([EXP, TC], F32)
SCRATCH["hin_s"] = ([128, EXPT], F32)
INPUT_SHAPES["rankmask"] = ([128, NCORES], F32)


def emit_F3(p: Prog, io: dict, st: contextlib.ExitStack):
    nc = p.nc
    v, s, t, g, sy = (p.q[e] for e in ("vector", "scalar", "tensor", "gpsimd", "sy" "nc"))
    identb = io["identb_sb"]
    onesrow = io["onesrow_sb"]
    p.nc_onesrow = onesrow
    NB = 512

    qdT = p.sb("f3_qdT", [128, I0T * D], BF16, st)
    sd = p.sb("f3_sd", [128, DT], F32, st)
    shd = p.sb("f3_shd", [128, DT], F32, st)
    wnat = p.sb("f3_wnat", [128, I0], F32, st)
    wfs = p.sb("f3_wfs", [128, I0], F32, st)
    wis = p.sb("f3_wis", [128, I0], mybir.dt.int32, st)
    qbf = p.sb("f3_qbf", [128, I0], BF16, st)
    scl = p.sb("f3_scl", [128, 4], F32, st)
    hch = p.sb("f3_hch", [128, I0T * NB], F32, st)
    hfs = p.sb("f3_hfs", [128, 8 * NB], F32, st)
    his = p.sb("f3_his", [128, 8 * NB], mybir.dt.int32, st)
    qh = p.sb("f3_qh", [128, I0T * NB], BF16, st)
    gt2 = [p.sb(f"f3_gt{i}", [128, NB], F32, st) for i in range(2)]
    f3ss, f3st, f3ctr = [], [], [0]
    rows = p.sb("f3_rows", [1, 8], F32, st)
    cols = p.sb("f3_cols", [128, 2], F32, st)

    p.dma("sync", rows[0:1, 0:1], io["clipd"][0:1, 0:1])
    p.dma("sync", rows[0:1, 1:2], io["gmaxh_s"][0:1, 0:1])
    p.wait("scalar", "dS")
    s.activation(rows[0:1, 2:3], rows[0:1, 0:1], AF.Sigmoid)
    p.edge("scalar", "vector")
    v.tensor_tensor(rows[0:1, 3:4], rows[0:1, 2:3], rows[0:1, 1:2], ALU.mult)
    p.vsync()
    v.reciprocal(rows[0:1, 4:5], rows[0:1, 3:4])
    p.vsync()
    v.tensor_single_scalar(rows[0:1, 5:6], rows[0:1, 4:5], QMAX, ALU.mult)
    v.tensor_single_scalar(rows[0:1, 6:7], rows[0:1, 3:4], 1.0 / QMAX, ALU.mult)
    p.vsync()
    bcast_scalar_col(p, cols[:, 0:1], rows[0:1, 5:6])
    bcast_scalar_col(p, cols[:, 1:2], rows[0:1, 6:7])

    for ot in range(DT):
        p.wait("sync", "vE")
        p.dma("sync", wnat[:], io["wd"][ot * 128:(ot + 1) * 128, :])
        p.wait("vector", "dS")
        v.reduce_max(scl[:, 0:1], wnat[:], axis=AX.X, apply_absolute_value=True)
        p.vsync()
        v.reciprocal(scl[:, 1:2], scl[:, 0:1])
        p.vsync()
        v.tensor_single_scalar(scl[:, 2:3], scl[:, 1:2], QMAX, ALU.mult)
        v.tensor_single_scalar(sd[:, ot:ot + 1], scl[:, 0:1], 1.0 / QMAX, ALU.mult)
        p.vsync()
        quant_sweeps(p, [(wnat[:], qbf[:])], scl[:, 2:3], [wis[:, 0:I0]], [wfs[:, 0:I0]])
        p.edge("vector", "tensor")
        for kt in range(I0T):
            t.transpose(p.psbf[:, 0:128], qbf[:, kt * 128:(kt + 1) * 128], identb[:])
            p.edge("tensor", "vector")
            v.tensor_copy(qdT[:, kt * D + ot * 128:kt * D + (ot + 1) * 128], p.psbf[:, 0:128])
            p.edge("vector", "tensor")
        p.edge("tensor", "vector")
    v.tensor_single_scalar(shd[:, 0:DT], sd[:, 0:DT], cols[:, 1:2], ALU.mult)
    p.vsync()

    for nb in range(4):
        p.wait("sync", "vE")
        p.wait("sync", "dS")
        for kt in range(I0T):
            p.dma("sync", hch[:, kt * NB:(kt + 1) * NB],
                  io["hmid_s"][kt * 128:(kt + 1) * 128, nb * NB:(nb + 1) * NB])
        p.wait("vector", "dS")
        for g0 in range(0, I0T, 8):
            n = min(8, I0T - g0)
            quant_sweeps(p,
                         [(hch[:, (g0 + j) * NB:(g0 + j + 1) * NB], qh[:, (g0 + j) * NB:(g0 + j + 1) * NB])
                          for j in range(n)],
                         cols[:, 0:1],
                         [his[:, j * NB:(j + 1) * NB] for j in range(n)],
                         [hfs[:, j * NB:(j + 1) * NB] for j in range(n)])
        p.edge("vector", "tensor")
        for ot in range(DT):
            j = f3ctr[0]; f3ctr[0] += 1
            bank = ot % 2
            if j >= 2:
                p.wait("tensor", "sE", f3ss[j - 2])
            for kt in range(I0T):
                t.matmul(p.ps(bank)[:, 0:NB], qdT[:, kt * D + ot * 128:kt * D + (ot + 1) * 128],
                         qh[:, kt * NB:(kt + 1) * NB], start=(kt == 0), stop=(kt == I0T - 1))
            p.edge("tensor", "scalar")
            if j >= 2:
                p.wait("scalar", "dS", f3st[j - 2])
            s.activation(gt2[j % 2][:], p.ps(bank)[:, 0:NB], AF.Silu, scale=shd[:, ot:ot + 1])
            f3ss.append(p.sig("scalar"))
            p.wait("sync", "sE", f3ss[j])
            p.dma("sync", io["g_s"][ot * 128:(ot + 1) * 128, nb * NB:(nb + 1) * NB], gt2[j % 2][:])
            f3st.append(p.val("dS"))
    p.wait("gpsimd", "dS")


PHASES["F3"] = dict(
    emit=emit_F3,
    inputs=["wd", "clipd"],
    consts=["ident", "identb", "onesrow"],
    scratch_r=["hmid_s", "gmaxh_s"],
    scratch_w=["g_s"],
)
SCRATCH["g_s"] = ([D, TC], F32)


def emit_F4a(p: Prog, io: dict, st: contextlib.ExitStack):
    nc = p.nc
    v, s, t, g, sy = (p.q[e] for e in ("vector", "scalar", "tensor", "gpsimd", "sync"))
    onescol = io["onescol_sb"]
    onesrow = io["onesrow_sb"]
    NB = 512

    dpT = p.sb("f4_dpT", [128, EXPT * D], BF16, st)
    ab = p.sb("f4_ab", [128, EXPT * NB], F32, st)
    bb = p.sb("f4_bb", [128, EXPT * NB], F32, st)
    hbf = p.sb("f4_hbf", [128, EXPT * NB], BF16, st)
    hin = p.sb("f4_hin", [128, EXPT], F32, st)
    x1c = p.sb("f4_x1c", [128, DT * NB], F32, st)
    sq = p.sb("f4_sq", [128, DT * NB], F32, st)
    mb = p.sb("f4_mb", [128, DT * NB], BF16, st)
    st1 = p.sb("f4_st1", [128, NB], F32, st)
    rowbuf = p.sb("f4_rows", [1, 512 * 6], F32, st)
    row = lambda i: rowbuf[0:1, 512 * i:512 * i + NB]

    p.wait("sync", "dP", io["_dpthr"]["dpT"])
    for et in range(EXPT):
        p.dma("sync", dpT[:, et * D:(et + 1) * D], io["dpT_b"][et * 128:(et + 1) * 128, :])
    p.dma("sync", hin[:, 0:EXPT], io["hin_s"][:, 0:EXPT])
    for e in ("sync", "tensor", "vector", "scalar"):
        p.wait(e, "dG")
        p.wait(e, "dS")

    for nb in range(4):
        p.wait("sync", "vE")
        p.wait("sync", "dS")
        for et in range(EXPT):
            p.dma("sync", ab[:, et * NB:(et + 1) * NB],
                  io["a_s"][et * 128:(et + 1) * 128, nb * NB:(nb + 1) * NB])
            p.dma("sync", bb[:, et * NB:(et + 1) * NB],
                  io["b_s"][et * 128:(et + 1) * 128, nb * NB:(nb + 1) * NB])
        for ct in range(DT):
            p.dma("sync", x1c[:, ct * NB:(ct + 1) * NB],
                  io["x1_s"][ct * 128:(ct + 1) * 128, nb * NB:(nb + 1) * NB])
        p.wait("vector", "dS")
        for et in range(EXPT):
            v.scalar_tensor_tensor(ab[:, et * NB:(et + 1) * NB], ab[:, et * NB:(et + 1) * NB],
                                   hin[:, et:et + 1], bb[:, et * NB:(et + 1) * NB], ALU.mult, ALU.add)
        p.vsync()
        for et in range(EXPT):
            v.tensor_copy(hbf[:, et * NB:(et + 1) * NB], ab[:, et * NB:(et + 1) * NB])
        p.vsync()
        for ot in range(DT):
            bank = ot % 2
            p.edge("vector", "tensor")
            for et in range(EXPT):
                t.matmul(p.ps(bank)[:, 0:NB], dpT[:, et * D + ot * 128:et * D + (ot + 1) * 128],
                         hbf[:, et * NB:(et + 1) * NB], start=(et == 0), stop=(et == EXPT - 1))
            p.edge("tensor", "vector")
            v.tensor_tensor(x1c[:, ot * NB:(ot + 1) * NB], p.ps(bank)[:, 0:NB],
                            x1c[:, ot * NB:(ot + 1) * NB], ALU.add)
            p.vsync()
        p.sig("vector")
        p.wait("sync", "vE")
        for ot in range(DT):
            p.dma("sync", io["x2_s"][ot * 128:(ot + 1) * 128, nb * NB:(nb + 1) * NB],
                  x1c[:, ot * NB:(ot + 1) * NB])
        p.edge("vector", "scalar")
        for ct in range(DT):
            s.activation(sq[:, ct * NB:(ct + 1) * NB], x1c[:, ct * NB:(ct + 1) * NB], AF.Square)
        p.edge("scalar", "tensor")
        for ct in range(DT):
            t.matmul(p.ps(2)[0:1, 0:NB], onescol[:], x1c[:, ct * NB:(ct + 1) * NB],
                     start=(ct == 0), stop=(ct == DT - 1))
        for ct in range(DT):
            t.matmul(p.ps(5)[0:1, 0:NB], onescol[:], sq[:, ct * NB:(ct + 1) * NB],
                     start=(ct == 0), stop=(ct == DT - 1))
        p.edge("tensor", "vector")
        ln_rows(p, row, p.ps(2)[0:1, 0:NB], p.ps(5)[0:1, 0:NB], want_r1=False)
        p.edge("vector", "tensor")
        t.matmul(p.ps(3)[:, 0:NB], onesrow[:], row(4), start=True, stop=True)
        t.matmul(p.ps(4)[:, 0:NB], onesrow[:], row(5), start=True, stop=True)
        p.edge("tensor", "vector")
        for ct in range(DT):
            v.tensor_tensor(sq[:, ct * NB:(ct + 1) * NB], x1c[:, ct * NB:(ct + 1) * NB],
                            p.ps(3)[:, 0:NB], ALU.mult)
        p.vsync()
        for ct in range(DT):
            v.tensor_tensor(sq[:, ct * NB:(ct + 1) * NB], sq[:, ct * NB:(ct + 1) * NB],
                            p.ps(4)[:, 0:NB], ALU.add)
        p.vsync()
        for ct in range(DT):
            v.tensor_scalar(sq[:, ct * NB:(ct + 1) * NB], sq[:, ct * NB:(ct + 1) * NB],
                            io["ln3g_sb"][:, ct:ct + 1], io["ln3b_sb"][:, ct:ct + 1],
                            ALU.mult, ALU.add)
        p.vsync()
        for ct in range(DT):
            v.tensor_copy(mb[:, ct * NB:(ct + 1) * NB], sq[:, ct * NB:(ct + 1) * NB])
        p.vsync()
        p.sig("vector")
        p.wait("sync", "vE")
        for ct in range(DT):
            p.dma("sync", io["m_s"][ct * 128:(ct + 1) * 128, nb * NB:(nb + 1) * NB],
                  mb[:, ct * NB:(ct + 1) * NB])
    p.wait("gpsimd", "dS")


PHASES["F4a"] = dict(
    emit=emit_F4a,
    inputs=["dpT"],
    consts=["onescol", "onesrow", "ln3g", "ln3b"],
    scratch_r=["a_s", "b_s", "hin_s", "x1_s"],
    scratch_w=["x2_s", "m_s"],
)
SCRATCH["x2_s"] = ([D, TC], F32)
SCRATCH["m_s"] = ([D, TC], BF16)


def emit_F4b(p: Prog, io: dict, st: contextlib.ExitStack):
    nc = p.nc
    v, s, t, g, sy = (p.q[e] for e in ("vector", "scalar", "tensor", "gpsimd", "sync"))
    NB = 512
    m1T = p.sb("f4b_m1T", [128, DT * 4096], BF16, st)
    mbc = p.sb("f4b_mbc", [128, DT * NB], BF16, st)
    gl2 = [p.sb(f"f4b_gl{i}", [128, NB], F32, st) for i in range(2)]
    glb2 = [p.sb(f"f4b_glb{i}", [128, NB], BF16, st) for i in range(2)]
    f4ss, f4vs, f4st, f4ctr = [], [], [], [0]
    p.wait("sync", "dP", io["_dpthr"]["m1T"])
    for ct in range(DT):
        p.dma("sync", m1T[:, ct * 4096:(ct + 1) * 4096], io["m1T_b"][ct * 128:(ct + 1) * 128, :])
    for e in ("sync", "tensor", "vector", "scalar"):
        p.wait(e, "dG")
    for nb in range(4):
        p.wait("sync", "vE")
        for ct in range(DT):
            p.dma("sync", mbc[:, ct * NB:(ct + 1) * NB],
                  io["m_s"][ct * 128:(ct + 1) * 128, nb * NB:(nb + 1) * NB])
        p.wait("tensor", "dS")
        for it in range(MLPT):
            j = f4ctr[0]; f4ctr[0] += 1
            bank = it % 2
            if j >= 2:
                p.wait("tensor", "sE", f4ss[j - 2])
            for ct in range(DT):
                t.matmul(p.ps(bank)[:, 0:NB], m1T[:, ct * 4096 + it * 128:ct * 4096 + (it + 1) * 128],
                         mbc[:, ct * NB:(ct + 1) * NB], start=(ct == 0), stop=(ct == DT - 1))
            p.edge("tensor", "scalar")
            s.activation(gl2[j % 2][:], p.ps(bank)[:, 0:NB], AF.Gelu, bias=io["mlpb1_sb"][:, it:it + 1])
            f4ss.append(p.sig("scalar"))
            p.wait("vector", "sE", f4ss[j])
            if j >= 2:
                p.wait("vector", "dS", f4st[j - 2])
            v.tensor_copy(glb2[j % 2][:], gl2[j % 2][:])
            f4vs.append(p.sig("vector"))
            p.wait("sync", "vE", f4vs[j])
            p.dma("sync", io["mid_s"][it * 128:(it + 1) * 128, nb * NB:(nb + 1) * NB], glb2[j % 2][:])
            f4st.append(p.val("dS"))
    p.wait("gpsimd", "dS")


PHASES["F4b"] = dict(
    emit=emit_F4b,
    inputs=["m1T"],
    consts=["mlpb1"],
    scratch_r=["m_s"],
    scratch_w=["mid_s"],
)
SCRATCH["mid_s"] = ([4096, TC], BF16)


def emit_F4c(p: Prog, io: dict, st: contextlib.ExitStack):
    nc = p.nc
    v, s, t, g, sy = (p.q[e] for e in ("vector", "scalar", "tensor", "gpsimd", "sync"))
    ident = io["ident_sb"]
    NB = 512
    m2T = p.sb("f4c_m2T", [128, MLPT * D], BF16, st)
    midc = p.sb("f4c_midc", [128, MLPT * NB], BF16, st)
    x2c = p.sb("f4c_x2c", [128, DT * NB], F32, st)
    gc = p.sb("f4c_gc", [128, DT * NB], F32, st)
    xt = p.sb("f4c_xt", [128, D], F32, st)
    xcl = p.sb("f4c_xcl", [128, DT * NB], F32, st)
    st1 = p.sb("f4c_st1", [128, NB], F32, st)
    p.wait("sync", "dP", io["_dpthr"]["m2T"])
    for kt in range(MLPT):
        p.dma("sync", m2T[:, kt * D:(kt + 1) * D], io["m2T_b"][kt * 128:(kt + 1) * 128, :])
    for e in ("sync", "tensor", "vector", "scalar"):
        p.wait(e, "dG")
    for nb in range(4):
        p.wait("sync", "vE")
        p.wait("sync", "dS")
        for kt in range(MLPT):
            p.dma("sync", midc[:, kt * NB:(kt + 1) * NB],
                  io["mid_s"][kt * 128:(kt + 1) * 128, nb * NB:(nb + 1) * NB])
        for ct in range(DT):
            p.dma("sync", x2c[:, ct * NB:(ct + 1) * NB],
                  io["x2_s"][ct * 128:(ct + 1) * 128, nb * NB:(nb + 1) * NB])
            p.dma("sync", gc[:, ct * NB:(ct + 1) * NB],
                  io["g_s"][ct * 128:(ct + 1) * 128, nb * NB:(nb + 1) * NB])
        for tt in range(4):
            p.wait("sync", "tE")
            p.dma("sync", xt[:], io["xs"][3 + nb * NB + tt * 128:3 + nb * NB + (tt + 1) * 128, :])
            p.wait("tensor", "dS")
            for ct in range(DT):
                bank = 6
                p.edge("vector", "tensor")
                t.transpose(p.ps(bank)[:, 0:128], xt[:, ct * 128:(ct + 1) * 128], ident[:])
                p.edge("tensor", "vector")
                v.tensor_copy(xcl[:, ct * NB + tt * 128:ct * NB + (tt + 1) * 128], p.ps(bank)[:, 0:128])
        p.wait("tensor", "dS")
        for ot in range(DT):
            bank = ot % 2
            p.edge("vector", "tensor")
            for kt in range(MLPT):
                t.matmul(p.ps(bank)[:, 0:NB], m2T[:, kt * D + ot * 128:kt * D + (ot + 1) * 128],
                         midc[:, kt * NB:(kt + 1) * NB], start=(kt == 0), stop=(kt == MLPT - 1))
            p.edge("tensor", "scalar")
            s.activation(st1[:], p.ps(bank)[:, 0:NB], AF.Identity, bias=io["mlpb2_sb"][:, ot:ot + 1])
            p.edge("scalar", "vector")
            v.tensor_tensor(x2c[:, ot * NB:(ot + 1) * NB], st1[:],
                            x2c[:, ot * NB:(ot + 1) * NB], ALU.add)
            p.vsync()
            v.tensor_tensor(x2c[:, ot * NB:(ot + 1) * NB], x2c[:, ot * NB:(ot + 1) * NB],
                            gc[:, ot * NB:(ot + 1) * NB], ALU.mult)
            p.vsync()
            v.tensor_tensor(x2c[:, ot * NB:(ot + 1) * NB], x2c[:, ot * NB:(ot + 1) * NB],
                            xcl[:, ot * NB:(ot + 1) * NB], ALU.add)
            p.vsync()
            p.sig("vector")
            p.wait("sync", "vE")
            p.dma("sync", io["xb1_s"][ot * 128:(ot + 1) * 128, nb * NB:(nb + 1) * NB],
                  x2c[:, ot * NB:(ot + 1) * NB])
    p.wait("gpsimd", "dS")


PHASES["F4c"] = dict(
    emit=emit_F4c,
    inputs=["m2T", "xs"],
    consts=["ident", "mlpb2"],
    scratch_r=["mid_s", "x2_s", "g_s"],
    scratch_w=["xb1_s"],
)
SCRATCH["xb1_s"] = ([D, TC], F32)


def emit_F5(p: Prog, io: dict, st: contextlib.ExitStack):
    nc = p.nc
    v, s, t, g, sy = (p.q[e] for e in ("vector", "scalar", "tensor", "gpsimd", "sync"))
    onescol = io["onescol_sb"]
    onesrow = io["onesrow_sb"]
    NB = 512
    GC = 1024 + 8 + 64

    wT = p.sb("f5_wT", [128, DT * 3072], BF16, st)
    rT = p.sb("f5_rT", [128, DT * 8], F32, st)
    xbc = p.sb("f5_xbc", [128, DT * NB], F32, st)
    sq = p.sb("f5_sq", [128, DT * NB], F32, st)
    xnb = p.sb("f5_xnb", [128, DT * NB], BF16, st)
    eqc = p.sb("f5_eqc", [128, DT * NB], F32, st)
    kxc = p.sb("f5_kxc", [128, DT * NB], F32, st)
    knc = p.sb("f5_knc", [128, DT * NB], F32, st)
    kst = p.sb("f5_kst", [128, DT], F32, st)
    ktm = p.sb("f5_ktm", [128, D], BF16, st)
    vtm = p.sb("f5_vtm", [128, D], BF16, st)
    ktf = p.sb("f5_ktf", [128, D], F32, st)
    snap = p.sb("f5_snap", [64, HEADS * HD], F32, st)
    zer = p.sb("f5_zer", [128, NB], F32, st)
    rout = p.sb("f5_rout", [128, 16 * 4], F32, st)
    prb = p.sb("f5_prb", [128, 8 * 4], F32, st)
    pcol = p.sb("f5_pcol", [128, 8], F32, st)
    iom8 = p.sb("f5_iom8", [128, 8], F32, st)
    rowbuf = p.sb("f5_rows", [1, 512 * 6], F32, st)
    ccsb = p.sb("f5_ccsb", [128, GC], F32, st)
    row = lambda i: rowbuf[0:1, 512 * i:512 * i + NB]

    cc_in = nc.dram_tensor("f5cc_in", [128, GC], F32)
    cc_out = nc.dram_tensor("f5cc_out", [NCORES * 128, GC], F32)

    p.wait("sync", "dP", io["_dpthr"]["l1wT"])
    for ct in range(DT):
        p.dma("sync", wT[:, ct * 3072:(ct + 1) * 3072], io["l1wT_b"][ct * 128:(ct + 1) * 128, :])
        p.dma("sync", rT[:, ct * 8:(ct + 1) * 8], io["routerT"][ct * 128:(ct + 1) * 128, :])
    g.memset(kst[:], 0.0)
    g.memset(zer[:], 0.0)
    g.memset(ccsb[:], 0.0)
    g.memset(snap[:], 0.0)
    p.sig("gpsimd")
    for e in ("sync", "tensor", "vector", "scalar"):
        p.wait(e, "dG")
        p.wait(e, "gE")
    v.tensor_single_scalar(iom8[:], io["iota8_sb"][:], -8.0, ALU.add)
    p.vsync()

    for nb in range(4):
        tok0 = nb * NB
        p.wait("sync", "vE")
        p.wait("sync", "dS")
        for ct in range(DT):
            p.dma("sync", xbc[:, ct * NB:(ct + 1) * NB],
                  io["xb1_s"][ct * 128:(ct + 1) * 128, tok0:tok0 + NB])
        p.wait("vector", "dS")
        p.wait("scalar", "dS")
        for ct in range(DT):
            s.activation(sq[:, ct * NB:(ct + 1) * NB], xbc[:, ct * NB:(ct + 1) * NB], AF.Square)
        p.edge("scalar", "tensor")
        for ct in range(DT):
            t.matmul(p.ps(2)[0:1, 0:NB], onescol[:], xbc[:, ct * NB:(ct + 1) * NB],
                     start=(ct == 0), stop=(ct == DT - 1))
        for ct in range(DT):
            t.matmul(p.ps(6)[0:1, 0:NB], onescol[:], sq[:, ct * NB:(ct + 1) * NB],
                     start=(ct == 0), stop=(ct == DT - 1))
        p.edge("tensor", "vector")
        ln_rows(p, row, p.ps(2)[0:1, 0:NB], p.ps(6)[0:1, 0:NB], want_r1=False)
        p.edge("vector", "tensor")
        t.matmul(p.ps(3)[:, 0:NB], onesrow[:], row(4), start=True, stop=True)
        t.matmul(p.ps(6)[:, 0:NB], onesrow[:], row(5), start=True, stop=True)
        p.edge("tensor", "vector")
        for ct in range(DT):
            v.tensor_tensor(xbc[:, ct * NB:(ct + 1) * NB], xbc[:, ct * NB:(ct + 1) * NB],
                            p.ps(3)[:, 0:NB], ALU.mult)
        p.vsync()
        for ct in range(DT):
            v.tensor_tensor(xbc[:, ct * NB:(ct + 1) * NB], xbc[:, ct * NB:(ct + 1) * NB],
                            p.ps(6)[:, 0:NB], ALU.add)
        p.vsync()
        for ct in range(DT):
            v.tensor_copy(xnb[:, ct * NB:(ct + 1) * NB], xbc[:, ct * NB:(ct + 1) * NB])
        p.vsync()
        p.sig("vector")
        p.wait("sync", "vE")
        for ct in range(DT):
            p.dma("sync", io["xn2_s"][ct * 128:(ct + 1) * 128, tok0:tok0 + NB],
                  xbc[:, ct * NB:(ct + 1) * NB])
            p.dma("sync", io["xn2b_s"][ct * 128:(ct + 1) * 128, tok0:tok0 + NB],
                  xnb[:, ct * NB:(ct + 1) * NB])
        p.edge("vector", "tensor")
        for ot in range(DT):
            bank = ot % 2
            if ot >= 2:
                p.wait("tensor", "sE")
            for ct in range(DT):
                t.matmul(p.ps(bank)[:, 0:NB], wT[:, ct * 3072 + ot * 128:ct * 3072 + (ot + 1) * 128],
                         xnb[:, ct * NB:(ct + 1) * NB], start=(ct == 0), stop=(ct == DT - 1))
            p.edge("tensor", "scalar")
            s.activation(eqc[:, ot * NB:(ot + 1) * NB], p.ps(bank)[:, 0:NB], AF.Exp, scale=0.125)
            p.sig("scalar")
        p.edge("scalar", "sync")
        for ot in range(DT):
            p.dma("sync", io["eq_s"][ot * 128:(ot + 1) * 128, tok0:tok0 + NB],
                  eqc[:, ot * NB:(ot + 1) * NB])
        p.edge("scalar", "tensor")
        for ct in range(DT):
            t.matmul(p.ps(6)[0:16, 0:NB], io["headmask_sb"][:, ct * 16:(ct + 1) * 16],
                     eqc[:, ct * NB:(ct + 1) * NB], start=(ct == 0), stop=(ct == DT - 1))
        p.edge("tensor", "vector")
        v.tensor_copy(sq[0:16, 0:NB], p.ps(6)[0:16, 0:NB])
        p.sig("vector")
        p.wait("sync", "vE")
        p.dma("sync", io["ssum_s"][0:16, tok0:tok0 + NB], sq[0:16, 0:NB])
        p.edge("vector", "tensor")
        for ot in range(DT):
            bank = ot % 2
            p.wait("tensor", "sE") if ot >= 2 else None
            for ct in range(DT):
                t.matmul(p.ps(bank)[:, 0:NB],
                         wT[:, ct * 3072 + (DT + ot) * 128:ct * 3072 + (DT + ot + 1) * 128],
                         xnb[:, ct * NB:(ct + 1) * NB], start=(ct == 0), stop=(ct == DT - 1))
            p.edge("tensor", "scalar")
            s.activation(kxc[:, ot * NB:(ot + 1) * NB], p.ps(bank)[:, 0:NB], AF.Exp, scale=0.125)
            p.sig("scalar")
        p.edge("scalar", "vector")
        for ot in range(DT):
            v.tensor_single_scalar(kxc[:, ot * NB:(ot + 1) * NB], kxc[:, ot * NB:(ot + 1) * NB],
                                   1e-6, ALU.add)
        p.vsync()
        for ot in range(DT):
            v.tensor_tensor_scan(knc[:, ot * NB:(ot + 1) * NB], zer[:],
                                 kxc[:, ot * NB:(ot + 1) * NB], kst[:, ot:ot + 1], ALU.add, ALU.add)
            p.vsync()
            v.tensor_copy(kst[:, ot:ot + 1], knc[:, (ot + 1) * NB - 1:(ot + 1) * NB])
            p.vsync()
        p.sig("vector")
        p.wait("sync", "vE")
        for ot in range(DT):
            p.dma("sync", io["kexp_s"][ot * 128:(ot + 1) * 128, tok0:tok0 + NB],
                  kxc[:, ot * NB:(ot + 1) * NB])
            p.dma("sync", io["kn_s"][ot * 128:(ot + 1) * 128, tok0:tok0 + NB],
                  knc[:, ot * NB:(ot + 1) * NB])
        for tt in range(4):
            chunk = nb * 4 + tt
            p.wait("tensor", "vE")
            for vh in range(2):
                bank = vh
                for ct in range(DT):
                    t.matmul(p.ps(bank)[:, 0:512],
                             xnb[:, ct * NB + tt * 128:ct * NB + (tt + 1) * 128],
                             wT[:, ct * 3072 + 2048 + vh * 512:ct * 3072 + 2048 + (vh + 1) * 512],
                             start=(ct == 0), stop=(ct == DT - 1))
            p.edge("tensor", "vector")
            p.wait("vector", "dS")
            v.tensor_copy(vtm[:, 0:512], p.ps(0)[:, 0:512])
            v.tensor_copy(vtm[:, 512:1024], p.ps(1)[:, 0:512])
            p.sig("vector")
            p.wait("sync", "vE")
            p.dma("sync", io["vt_s"][tok0 + tt * 128:tok0 + (tt + 1) * 128, :], vtm[:])
            p.edge("vector", "tensor")
            for vh in range(2):
                bank = vh
                for ct in range(DT):
                    t.matmul(p.ps(bank)[:, 0:512],
                             xnb[:, ct * NB + tt * 128:ct * NB + (tt + 1) * 128],
                             wT[:, ct * 3072 + 1024 + vh * 512:ct * 3072 + 1024 + (vh + 1) * 512],
                             start=(ct == 0), stop=(ct == DT - 1))
            p.edge("tensor", "scalar")
            s.activation(ktf[:, 0:512], p.ps(0)[:, 0:512], AF.Exp, scale=0.125)
            s.activation(ktf[:, 512:1024], p.ps(1)[:, 0:512], AF.Exp, scale=0.125)
            p.edge("scalar", "vector")
            v.tensor_single_scalar(ktf[:], ktf[:], 1e-6, ALU.add)
            p.vsync()
            v.tensor_copy(ktm[:], ktf[:])
            p.vsync()
            p.sig("vector")
            p.wait("sync", "vE")
            p.dma("sync", io["ket_s"][tok0 + tt * 128:tok0 + (tt + 1) * 128, :], ktm[:])
            p.wait("vector", "dS")
            p.edge("vector", "tensor")
            for h in range(HEADS):
                bank = 4 + h // 8
                t.matmul(p.ps(bank)[0:64, (h % 8) * 64:(h % 8 + 1) * 64],
                         ktm[:, h * 64:(h + 1) * 64], vtm[:, h * 64:(h + 1) * 64],
                         start=True, stop=True)
            p.edge("tensor", "vector")
            v.tensor_tensor(snap[0:64, 0:512], snap[0:64, 0:512], p.ps(4)[0:64, 0:512], ALU.add)
            v.tensor_tensor(snap[0:64, 512:1024], snap[0:64, 512:1024], p.ps(5)[0:64, 0:512], ALU.add)
            p.vsync()
            p.sig("vector")
            p.wait("sync", "vE")
            p.dma("sync", io["snap_s"][chunk * 64:(chunk + 1) * 64, :], snap[0:64, :])
            p.wait("vector", "dS")
            p.edge("vector", "tensor")
        for tt in range(4):
            p.wait("tensor", "vE")
            for ct in range(DT):
                t.matmul(p.ps(6)[0:128, 0:8], xbc[:, ct * NB + tt * 128:ct * NB + (tt + 1) * 128],
                         rT[:, ct * 8:(ct + 1) * 8], start=(ct == 0), stop=(ct == DT - 1))
            p.edge("tensor", "vector")
            P0 = prb[:, 0:8]; P2 = prb[:, 8:16]; MSK = prb[:, 16:24]; SCR = prb[:, 24:32]
            v.reduce_max(pcol[:, 0:1], p.ps(6)[0:128, 0:8], axis=AX.X)
            p.vsync()
            v.tensor_single_scalar(P0, p.ps(6)[0:128, 0:8], pcol[:, 0:1], ALU.subtract)
            p.edge("vector", "scalar")
            s.activation(P0, P0, AF.Exp)
            p.edge("scalar", "vector")
            v.reduce_sum(pcol[:, 1:2], P0, axis=AX.X)
            p.vsync()
            v.reciprocal(pcol[:, 2:3], pcol[:, 1:2])
            p.vsync()
            v.tensor_single_scalar(P0, P0, pcol[:, 2:3], ALU.mult)
            p.vsync()
            v.reduce_max(pcol[:, 3:4], P0, axis=AX.X)
            p.vsync()
            v.tensor_single_scalar(MSK, P0, pcol[:, 3:4], ALU.is_ge)
            p.vsync()
            v.tensor_tensor(SCR, MSK, iom8[:], ALU.mult)
            p.vsync()
            v.tensor_single_scalar(SCR, SCR, 8.0, ALU.add)
            p.vsync()
            v.tensor_reduce(pcol[:, 4:5], SCR, axis=AX.X, op=ALU.min)
            p.vsync()
            v.tensor_single_scalar(MSK, io["iota8_sb"][:], pcol[:, 4:5], ALU.is_equal)
            p.vsync()
            v.scalar_tensor_tensor(P2, MSK, -2.0, P0, ALU.mult, ALU.add)
            p.vsync()
            v.reduce_max(pcol[:, 5:6], P2, axis=AX.X)
            p.vsync()
            v.tensor_single_scalar(MSK, P2, pcol[:, 5:6], ALU.is_ge)
            p.vsync()
            v.tensor_tensor(SCR, MSK, iom8[:], ALU.mult)
            p.vsync()
            v.tensor_single_scalar(SCR, SCR, 8.0, ALU.add)
            p.vsync()
            v.tensor_reduce(pcol[:, 6:7], SCR, axis=AX.X, op=ALU.min)
            p.vsync()
            v.tensor_tensor(pcol[:, 7:8], pcol[:, 3:4], pcol[:, 5:6], ALU.add)
            p.vsync()
            v.tensor_single_scalar(pcol[:, 7:8], pcol[:, 7:8], EPS_MOE, ALU.max)
            p.vsync()
            v.reciprocal(pcol[:, 7:8], pcol[:, 7:8])
            p.vsync()
            c0 = tt * 4
            v.tensor_tensor(rout[:, c0:c0 + 1], pcol[:, 3:4], pcol[:, 7:8], ALU.mult)
            v.tensor_tensor(rout[:, c0 + 1:c0 + 2], pcol[:, 5:6], pcol[:, 7:8], ALU.mult)
            v.tensor_copy(rout[:, c0 + 2:c0 + 3], pcol[:, 4:5])
            v.tensor_copy(rout[:, c0 + 3:c0 + 4], pcol[:, 6:7])
            p.vsync()
        p.sig("vector")
        p.wait("sync", "vE")
        p.dma("sync", io["rout_s"][:, nb * 16:(nb + 1) * 16], rout[:, 0:16])

    p.edge("vector", "gpsimd")
    v.tensor_copy(ccsb[0:64, 0:1024], snap[0:64, 0:1024])
    v.tensor_copy(ccsb[:, 1024:1024 + DT], kst[:, 0:DT])
    p.sig("vector")
    p.wait("sync", "vE")
    p.wait("sync", "dS")
    for nb4 in range(4):
        p.dma("sync", ccsb[:, 1032 + nb4 * 16:1032 + (nb4 + 1) * 16],
              io["rout_s"][:, nb4 * 16:(nb4 + 1) * 16])
    p.wait("sync", "vE")
    p.wait("sync", "dS")
    p.dma("sync", cc_in.ap(), ccsb[:, :])
    p.wait("gpsimd", "dS")
    p.q["gpsimd"].rec("collective_compute", "AllGather", ALU.bypass, replica_groups=RG,
                      ins=[cc_in.ap().opt()], outs=[cc_out.ap().opt()]).then_inc(*p.bump("cc", 1))
    p.wait("sync", "cc")
    for r in range(NCORES):
        p.dma("sync", ccsb[:, :], cc_out[r * 128:(r + 1) * 128, :])
        p.wait("sync", "dS")
        p.dma("sync", io["ggath_s"][r * 128:(r + 1) * 128, :], ccsb[:, :])
        p.wait("sync", "dS")


PHASES["F5"] = dict(
    emit=emit_F5,
    inputs=["l1wT", "routerT"],
    consts=["onescol", "onesrow", "headmask", "iota8"],
    scratch_r=["xb1_s"],
    scratch_w=["xn2_s", "xn2b_s", "eq_s", "kexp_s", "kn_s", "ssum_s",
               "vt_s", "ket_s", "snap_s", "rout_s", "ggath_s"],
)
SCRATCH["xn2_s"] = ([D, TC], F32)
SCRATCH["xn2b_s"] = ([D, TC], BF16)
SCRATCH["eq_s"] = ([D, TC], F32)
SCRATCH["kexp_s"] = ([D, TC], F32)
SCRATCH["kn_s"] = ([D, TC], F32)
SCRATCH["ssum_s"] = ([16, TC], F32)
SCRATCH["vt_s"] = ([TC, D], BF16)
SCRATCH["ket_s"] = ([TC, D], BF16)
SCRATCH["snap_s"] = ([16 * 64, HEADS * HD], F32)
SCRATCH["rout_s"] = ([128, 64], F32)
SCRATCH["ggath_s"] = ([NCORES * 128, 1024 + 8 + 64], F32)


def emit_F6a(p: Prog, io: dict, st: contextlib.ExitStack):
    nc = p.nc
    v, s, t, g, sy = (p.q[e] for e in ("vector", "scalar", "tensor", "gpsimd", "sync"))
    onescol = io["onescol_sb"]
    NB = 512
    GC = 1024 + 8 + 64

    wo = p.sb("f6_wo", [128, DT * D], BF16, st)
    koff = p.sb("f6_koff", [128, DT], F32, st)
    kwrk = p.sb("f6_kwrk", [128, DT], F32, st)
    sin = p.sb("f6_sin", [64, HEADS * HD], F32, st)
    swrk = p.sb("f6_swrk", [64, HEADS * HD], F32, st)
    sci = p.sb("f6_sci", [64, HEADS * HD], F32, st)
    scib = p.sb("f6_scib", [128, HEADS * HD // 2], BF16, st)
    eqc = p.sb("f6_eqc", [128, DT * NB], F32, st)
    knc = p.sb("f6_knc", [128, DT * NB], F32, st)
    kxb = p.sb("f6_kxb", [128, DT * NB], BF16, st)
    qsb = p.sb("f6_qsb", [128, DT * NB], BF16, st)
    ssc = p.sb("f6_ssc", [16, NB], F32, st)
    vtt = p.sb("f6_vtt", [128, 4 * D], BF16, st)
    attf = p.sb("f6_attf", [128, 128], F32, st)
    attb = p.sb("f6_attb", [128, 128], BF16, st)
    ycl = p.sb("f6_ycl", [128, DT * NB], BF16, st)
    s1c = p.sb("f6_s1c", [128, NB], F32, st)

    p.wait("sync", "dP", io["_dpthr"]["l1wo"])
    for ct in range(DT):
        p.dma("sync", wo[:, ct * D:(ct + 1) * D], io["l1wo_b"][ct * 128:(ct + 1) * 128, :])
    g.memset(koff[:], 0.0)
    g.memset(sin[:], 0.0)
    p.sig("gpsimd")
    for e in ("sync", "tensor", "vector", "scalar"):
        p.wait(e, "dG")
        p.wait(e, "gE")
    for r in range(NCORES):
        p.wait("sync", "vE")
        p.dma("sync", kwrk[:, 0:DT], io["ggath_s"][r * 128:(r + 1) * 128, 1024:1024 + DT])
        p.dma("sync", swrk[0:64, :], io["ggath_s"][r * 128:r * 128 + 64, 0:1024])
        p.wait("vector", "dS")
        v.scalar_tensor_tensor(koff[:, 0:DT], kwrk[:, 0:DT], io["rankmask_sb"][:, r:r + 1],
                               koff[:, 0:DT], ALU.mult, ALU.add)
        p.vsync()
        v.scalar_tensor_tensor(sin[0:64, :], swrk[0:64, :], io["rankmask_sb"][0:64, r:r + 1],
                               sin[0:64, :], ALU.mult, ALU.add)
        p.vsync()

    for nb in range(4):
        tok0 = nb * NB
        p.wait("sync", "vE")
        p.wait("sync", "dS")
        for ct in range(DT):
            p.dma("sync", eqc[:, ct * NB:(ct + 1) * NB],
                  io["eq_s"][ct * 128:(ct + 1) * 128, tok0:tok0 + NB])
            p.dma("sync", knc[:, ct * NB:(ct + 1) * NB],
                  io["kn_s"][ct * 128:(ct + 1) * 128, tok0:tok0 + NB])
        p.dma("sync", ssc[0:16, 0:NB], io["ssum_s"][0:16, tok0:tok0 + NB])
        for tt in range(4):
            p.dma("sync", vtt[:, tt * D:(tt + 1) * D], io["vt_s"][tok0 + tt * 128:tok0 + (tt + 1) * 128, :])
        p.wait("vector", "dS")
        p.wait("scalar", "dS")
        for ct in range(DT):
            v.tensor_single_scalar(knc[:, ct * NB:(ct + 1) * NB], knc[:, ct * NB:(ct + 1) * NB],
                                   koff[:, ct:ct + 1], ALU.add)
        p.vsync()
        p.edge("vector", "scalar")
        for ct in range(DT):
            s.activation(knc[:, ct * NB:(ct + 1) * NB], knc[:, ct * NB:(ct + 1) * NB], AF.Ln)
        for ct in range(DT):
            s.activation(knc[:, ct * NB:(ct + 1) * NB], knc[:, ct * NB:(ct + 1) * NB], AF.Exp, scale=-1.0)
        s.activation(ssc[0:16, 0:NB], ssc[0:16, 0:NB], AF.Ln)
        s.activation(ssc[0:16, 0:NB], ssc[0:16, 0:NB], AF.Exp, scale=-1.0)
        p.edge("scalar", "vector")
        for ct in range(DT):
            v.tensor_tensor(eqc[:, ct * NB:(ct + 1) * NB], eqc[:, ct * NB:(ct + 1) * NB],
                            knc[:, ct * NB:(ct + 1) * NB], ALU.mult)
        p.vsync()
        p.edge("vector", "tensor")
        for ct in range(DT):
            t.matmul(p.ps(6)[:, 0:NB], io["expmask_sb"][0:16, ct * 128:(ct + 1) * 128],
                     ssc[0:16, 0:NB], start=True, stop=True)
            p.edge("tensor", "vector")
            v.tensor_tensor(eqc[:, ct * NB:(ct + 1) * NB], eqc[:, ct * NB:(ct + 1) * NB],
                            p.ps(6)[:, 0:NB], ALU.mult)
            p.vsync()
            p.edge("vector", "tensor")
        for ct in range(DT):
            v.tensor_copy(qsb[:, ct * NB:(ct + 1) * NB], eqc[:, ct * NB:(ct + 1) * NB])
        p.vsync()
        p.wait("sync", "vE")
        for ct in range(DT):
            p.dma("sync", eqc[:, ct * NB:(ct + 1) * NB],
                  io["kexp_s"][ct * 128:(ct + 1) * 128, tok0:tok0 + NB])
        p.wait("vector", "dS")
        for ct in range(DT):
            v.tensor_copy(kxb[:, ct * NB:(ct + 1) * NB], eqc[:, ct * NB:(ct + 1) * NB])
        p.vsync()
        for c in range(4):
            gc = nb * 4 + c
            p.wait("sync", "vE")
            if gc > 0:
                p.dma("sync", swrk[0:64, :], io["snap_s"][(gc - 1) * 64:gc * 64, :])
                p.wait("vector", "dS")
                v.tensor_tensor(sci[0:64, :], sin[0:64, :], swrk[0:64, :], ALU.add)
            else:
                v.tensor_copy(sci[0:64, :], sin[0:64, :])
            p.vsync()
            for h in range(HEADS):
                r0h = (h % 2) * 64
                v.tensor_copy(scib[r0h:r0h + 64, (h // 2) * 64:(h // 2 + 1) * 64],
                              sci[0:64, h * 64:(h + 1) * 64])
            p.vsync()
            p.edge("vector", "tensor")
            for h in range(HEADS):
                ct = h // 2
                r0 = (h % 2) * 64
                colc = ct * NB + c * 128
                t.matmul(p.ps(0)[0:128, 0:128], kxb[r0:r0 + 64, colc:colc + 128],
                         qsb[r0:r0 + 64, colc:colc + 128], start=True, stop=True)
                p.edge("tensor", "vector")
                v.tensor_tensor(attf[:], p.ps(0)[0:128, 0:128], io["causal_sb"][:], ALU.mult)
                p.vsync()
                v.tensor_copy(attb[:], attf[:])
                p.vsync()
                p.edge("vector", "tensor")
                t.matmul(p.ps(2)[0:64, 0:128], vtt[:, c * D + h * 64:c * D + (h + 1) * 64],
                         attb[:], start=True, stop=False)
                t.matmul(p.ps(2)[0:64, 0:128], scib[r0:r0 + 64, ct * 64:(ct + 1) * 64],
                         qsb[r0:r0 + 64, colc:colc + 128], start=False, stop=True)
                p.edge("tensor", "vector")
                v.tensor_copy(ycl[r0:r0 + 64, colc:colc + 128], p.ps(2)[0:64, 0:128])
                p.vsync()
                p.edge("vector", "tensor")
        for ot in range(DT):
            bank = 3 + ot % 2
            p.edge("vector", "tensor")
            for ct in range(DT):
                t.matmul(p.ps(bank)[:, 0:NB], wo[:, ct * D + ot * 128:ct * D + (ot + 1) * 128],
                         qsb[:, 0:NB] if False else ycl[:, ct * NB:(ct + 1) * NB],
                         start=(ct == 0), stop=(ct == DT - 1))
            p.edge("tensor", "vector")
            v.tensor_copy(s1c[:], p.ps(bank)[:, 0:NB])
            p.sig("vector")
            p.wait("sync", "vE")
            p.dma("sync", io["s1_s"][ot * 128:(ot + 1) * 128, tok0:tok0 + NB], s1c[:])
            p.wait("vector", "dS")
    p.wait("gpsimd", "dS")


PHASES["F6a"] = dict(
    emit=emit_F6a,
    inputs=["l1wo"],
    consts=["onescol", "expmask", "causal", "rankmask"],
    scratch_r=["ggath_s", "eq_s", "kexp_s", "kn_s", "ssum_s", "vt_s", "snap_s"],
    scratch_w=["s1_s"],
)
SCRATCH["s1_s"] = ([D, TC], F32)


def emit_F6b(p: Prog, io: dict, st: contextlib.ExitStack):
    nc = p.nc
    v, s, t, g, sy = (p.q[e] for e in ("vector", "scalar", "tensor", "gpsimd", "sync"))
    onescol = io["onescol_sb"]
    onesrow = io["onesrow_sb"]
    ident = io["ident_sb"]
    NB = 512
    HALF = 1024

    grt = p.sb("m_grt", [128, NCORES * 64], F32, st)
    Wv = p.sb("m_Wv", [128, 256], F32, st)
    Ev = p.sb("m_Ev", [128, 256], F32, st)
    EM = p.sb("m_EM", [128, NEXP * 256], F32, st)
    KW = p.sb("m_KW", [128, 256], F32, st)
    WS = p.sb("m_WS", [128, 128], F32, st)
    CNT = p.sb("m_CNT", [128, 8], F32, st)
    MB = p.sb("m_MB", [128, 24], F32, st)
    rows8 = p.sb("m_rows8", [1, 8 * 6], F32, st)
    scr = p.sb("m_scr", [128, 256], F32, st)
    wme = p.sb("m_wme", [128, NEXP * 16], F32, st)
    wmT = p.sb("m_wmT", [16, 128], F32, st)
    wrt = p.sb("m_wrt", [1, HALF], F32, st)
    wb = p.sb("m_wb", [128, 2 * NB], F32, st)
    xnb = p.sb("m_xnb", [128, DT * HALF], BF16, st)
    wbuf = [p.sb(f"m_wbuf{i}", [128, DT * 128], BF16, st) for i in range(4)]
    pbuf = [p.sb(f"m_pbuf{i}", [128, MLPT * 128], BF16, st) for i in range(2)]
    zbuf = p.sb("m_zbuf", [128, MLPT * HALF], BF16, st)
    zt2 = [p.sb(f"m_zt{i}", [128, NB], F32, st) for i in range(2)]
    moacc = p.sb("m_moacc", [128, DT * HALF], F32, st)
    aux1 = p.sb("m_aux1", [128, HALF], F32, st)
    aux2 = p.sb("m_aux2", [128, HALF], F32, st)
    otm = p.sb("m_otm", [128, D], F32, st)

    wmrow_d = nc.dram_tensor("m_wmrow_d", [NEXP, 16, 128], F32)

    p.wait("sync", "dP", io["_dpthr"]["postT"])
    p.wait("scalar", "dP", io["_dpthr"]["postT"])
    for r in range(NCORES):
        p.dma("sync", grt[:, r * 64:(r + 1) * 64], io["ggath_s"][r * 128:(r + 1) * 128, 1032:1096])
    p.wait("vector", "dS")
    for ci in range(NCORES):
        src = grt[:, ci * 64:(ci + 1) * 64].rearrange("p (a f) -> p a f", f=4)
        for f, dst in ((0, Wv[:, ci * 32:ci * 32 + 16]), (1, Wv[:, ci * 32 + 16:ci * 32 + 32]),
                       (2, Ev[:, ci * 32:ci * 32 + 16]), (3, Ev[:, ci * 32 + 16:ci * 32 + 32])):
            v.tensor_copy(dst.rearrange("p (a f) -> p a f", f=1), src[:, :, f:f + 1])
    p.vsync()
    for e in range(NEXP):
        v.tensor_single_scalar(EM[:, e * 256:(e + 1) * 256], Ev[:], float(e), ALU.is_equal)
    p.vsync()
    g.memset(rows8[0:1, 0:8], -1.0)
    g.memset(rows8[0:1, 8:16], 1.0)
    p.sig("gpsimd")
    p.wait("vector", "gE")
    for itr in range(30):
        v.tensor_tensor(rows8[0:1, 16:24], rows8[0:1, 0:8], rows8[0:1, 8:16], ALU.add)
        p.vsync()
        v.tensor_single_scalar(rows8[0:1, 16:24], rows8[0:1, 16:24], 0.5, ALU.mult)
        p.vsync()
        p.edge("vector", "tensor")
        t.matmul(p.ps(6)[:, 0:8], onesrow[:], rows8[0:1, 16:24], start=True, stop=True)
        p.edge("tensor", "vector")
        v.tensor_copy(MB[:, 16:24], p.ps(6)[:, 0:8])
        p.vsync()
        for e in range(NEXP):
            v.tensor_single_scalar(scr[:], Wv[:], MB[:, 16 + e:17 + e], ALU.is_gt)
            p.vsync()
            v.tensor_tensor(scr[:], scr[:], EM[:, e * 256:(e + 1) * 256], ALU.mult)
            p.vsync()
            v.reduce_sum(CNT[:, e:e + 1], scr[:], axis=AX.X)
            p.vsync()
        p.edge("vector", "tensor")
        t.matmul(p.ps(6)[0:1, 0:8], onescol[:], CNT[:, 0:8], start=True, stop=True)
        p.edge("tensor", "vector")
        v.tensor_single_scalar(rows8[0:1, 32:40], p.ps(6)[0:1, 0:8], float(CAPACITY), ALU.is_le)
        p.vsync()
        v.tensor_tensor(rows8[0:1, 40:48], rows8[0:1, 16:24], rows8[0:1, 8:16], ALU.subtract)
        p.vsync()
        v.tensor_tensor(rows8[0:1, 40:48], rows8[0:1, 40:48], rows8[0:1, 32:40], ALU.mult)
        p.vsync()
        v.tensor_tensor(rows8[0:1, 8:16], rows8[0:1, 8:16], rows8[0:1, 40:48], ALU.add)
        p.vsync()
        v.tensor_tensor(rows8[0:1, 40:48], rows8[0:1, 16:24], rows8[0:1, 0:8], ALU.subtract)
        p.vsync()
        v.tensor_tensor(scr[0:1, 0:8], rows8[0:1, 40:48], rows8[0:1, 32:40], ALU.mult)
        p.vsync()
        v.tensor_tensor(rows8[0:1, 0:8], rows8[0:1, 0:8], rows8[0:1, 40:48], ALU.add)
        p.vsync()
        v.tensor_tensor(rows8[0:1, 0:8], rows8[0:1, 0:8], scr[0:1, 0:8], ALU.subtract)
        p.vsync()
    p.edge("vector", "tensor")
    t.matmul(p.ps(6)[:, 0:8], onesrow[:], rows8[0:1, 8:16], start=True, stop=True)
    p.edge("tensor", "vector")
    v.tensor_copy(MB[:, 0:8], p.ps(6)[:, 0:8])
    p.vsync()
    p.edge("vector", "gpsimd")
    g.memset(scr[:], 0.0)
    p.sig("gpsimd")
    p.wait("vector", "gE")
    for e in range(NEXP):
        v.scalar_tensor_tensor(scr[:], EM[:, e * 256:(e + 1) * 256], MB[:, e:e + 1], scr[:],
                               ALU.mult, ALU.add)
        p.vsync()
    v.tensor_tensor(KW[:], Wv[:], scr[:], ALU.is_gt)
    p.vsync()
    v.tensor_tensor(KW[:], KW[:], Wv[:], ALU.mult)
    p.vsync()
    for ci in range(NCORES):
        v.tensor_tensor(WS[:, ci * 16:(ci + 1) * 16], KW[:, ci * 32:ci * 32 + 16],
                        KW[:, ci * 32 + 16:ci * 32 + 32], ALU.add)
    p.vsync()
    v.tensor_single_scalar(WS[:], WS[:], EPS_MOE, ALU.max)
    p.edge("vector", "scalar")
    s.activation(WS[:], WS[:], AF.Ln)
    s.activation(WS[:], WS[:], AF.Exp, scale=-1.0)
    p.edge("scalar", "vector")
    for ci in range(NCORES):
        v.tensor_tensor(KW[:, ci * 32:ci * 32 + 16], KW[:, ci * 32:ci * 32 + 16],
                        WS[:, ci * 16:(ci + 1) * 16], ALU.mult)
        v.tensor_tensor(KW[:, ci * 32 + 16:ci * 32 + 32], KW[:, ci * 32 + 16:ci * 32 + 32],
                        WS[:, ci * 16:(ci + 1) * 16], ALU.mult)
    p.vsync()
    MW1 = scr[:, 0:16]; MW2 = scr[:, 16:32]; ME1 = scr[:, 32:48]; ME2 = scr[:, 48:64]
    p.edge("vector", "gpsimd")
    g.memset(scr[:, 0:64], 0.0)
    p.sig("gpsimd")
    p.wait("vector", "gE")
    for ci in range(NCORES):
        v.scalar_tensor_tensor(MW1, KW[:, ci * 32:ci * 32 + 16], io["rankhot_sb"][:, ci:ci + 1],
                               MW1, ALU.mult, ALU.add)
        p.vsync()
        v.scalar_tensor_tensor(MW2, KW[:, ci * 32 + 16:ci * 32 + 32], io["rankhot_sb"][:, ci:ci + 1],
                               MW2, ALU.mult, ALU.add)
        p.vsync()
        v.scalar_tensor_tensor(ME1, Ev[:, ci * 32:ci * 32 + 16], io["rankhot_sb"][:, ci:ci + 1],
                               ME1, ALU.mult, ALU.add)
        p.vsync()
        v.scalar_tensor_tensor(ME2, Ev[:, ci * 32 + 16:ci * 32 + 32], io["rankhot_sb"][:, ci:ci + 1],
                               ME2, ALU.mult, ALU.add)
        p.vsync()
    for e in range(NEXP):
        v.tensor_single_scalar(wme[:, e * 16:(e + 1) * 16], ME1, float(e), ALU.is_equal)
        p.vsync()
        v.tensor_tensor(wme[:, e * 16:(e + 1) * 16], wme[:, e * 16:(e + 1) * 16], MW1, ALU.mult)
        p.vsync()
        v.tensor_single_scalar(scr[:, 64:80], ME2, float(e), ALU.is_equal)
        p.vsync()
        v.tensor_tensor(scr[:, 64:80], scr[:, 64:80], MW2, ALU.mult)
        p.vsync()
        v.tensor_tensor(wme[:, e * 16:(e + 1) * 16], wme[:, e * 16:(e + 1) * 16], scr[:, 64:80], ALU.add)
        p.vsync()
        p.edge("vector", "tensor")
        t.transpose(p.ps(4)[0:16, 0:128], wme[:, e * 16:(e + 1) * 16], ident[:])
        p.edge("tensor", "vector")
        v.tensor_copy(wmT[0:16, 0:128], p.ps(4)[0:16, 0:128])
        p.sig("vector")
        p.wait("sync", "vE")
        p.dma("sync", wmrow_d[e, :, :], wmT[0:16, 0:128])
        p.wait("sync", "dS")
        p.wait("vector", "dS")

    for h2 in range(2):
        tok0 = h2 * HALF
        p.wait("sync", "vE")
        p.wait("sync", "dS")
        for ct in range(DT):
            p.dma("sync", xnb[:, ct * HALF:(ct + 1) * HALF],
                  io["xn2b_s"][ct * 128:(ct + 1) * 128, tok0:tok0 + HALF])
        p.wait("gpsimd", "tE")
        g.memset(moacc[:], 0.0)
        p.sig("gpsimd")
        p.wait("vector", "gE")
        p.wait("vector", "dS")
        p.wait("tensor", "dS")
        p.wait("tensor", "gE")
        for e in range(NEXP):
            p.wait("sync", "vE")
            p.dma("sync", wrt[0:1, 0:HALF],
                  wmrow_d.ap().rearrange("e a b -> e (a b)")[e:e + 1, tok0:tok0 + HALF])
            p.wait("tensor", "dS")
            for b2 in range(2):
                t.matmul(p.ps(6)[:, 0:NB], onesrow[:], wrt[0:1, b2 * NB:(b2 + 1) * NB],
                         start=True, stop=True)
                p.edge("tensor", "vector")
                v.tensor_copy(wb[:, b2 * NB:(b2 + 1) * NB], p.ps(6)[:, 0:NB])
                p.vsync()
                p.edge("vector", "tensor")
            zsig = []
            ssig = []
            for it in range(MLPT):
                if e + it > 0:
                    p.wait("sync", "tE")
                    p.wait("scalar", "tE")
                p.dma("sync", wbuf[it % 2][:],
                      io["preT_b"][e, it].rearrange("p c m -> p (c m)"))
                p.dma("scalar", wbuf[2 + it % 2][:],
                      io["gateT_b"][e, it].rearrange("p c m -> p (c m)"))
                p.wait("tensor", "dS")
                p.wait("tensor", "dC")
                for b2 in range(2):
                    j = it * 2 + b2
                    bA = j % 2
                    bB = 2 + j % 2
                    if j >= 2:
                        p.wait("tensor", "vE", zsig[j - 2])
                        p.wait("tensor", "sE", ssig[j - 2])
                    for ct in range(DT):
                        t.matmul(p.ps(bA)[:, 0:NB], wbuf[it % 2][:, ct * 128:(ct + 1) * 128],
                                 xnb[:, ct * HALF + b2 * NB:ct * HALF + (b2 + 1) * NB],
                                 start=(ct == 0), stop=(ct == DT - 1))
                    for ct in range(DT):
                        t.matmul(p.ps(bB)[:, 0:NB], wbuf[2 + it % 2][:, ct * 128:(ct + 1) * 128],
                                 xnb[:, ct * HALF + b2 * NB:ct * HALF + (b2 + 1) * NB],
                                 start=(ct == 0), stop=(ct == DT - 1))
                    p.edge("tensor", "scalar")
                    s.activation(zt2[j % 2][:], p.ps(bB)[:, 0:NB], AF.Silu)
                    ssig.append(p.sig("scalar"))
                    p.wait("vector", "sE", ssig[j])
                    v.tensor_tensor(zt2[j % 2][:], zt2[j % 2][:], p.ps(bA)[:, 0:NB], ALU.mult)
                    p.vsync()
                    v.tensor_copy(zbuf[:, it * HALF + b2 * NB:it * HALF + (b2 + 1) * NB], zt2[j % 2][:])
                    zsig.append(p.sig("vector"))
            msig = []
            for ot in range(DT):
                pb = pbuf[ot % 2]
                p.wait("scalar", "tE")
                p.dma("scalar", pb[:],
                      io["postT_b"][e, ot].rearrange("p c m -> p (c m)"))
                p.wait("tensor", "dC")
                for b2 in range(2):
                    j = ot * 2 + b2
                    bank = j % 2
                    if j >= 2:
                        p.wait("tensor", "vE", msig[j - 2])
                    for kt in range(MLPT):
                        t.matmul(p.ps(bank)[:, 0:NB], pb[:, kt * 128:(kt + 1) * 128],
                                 zbuf[:, kt * HALF + b2 * NB:kt * HALF + (b2 + 1) * NB],
                                 start=(kt == 0), stop=(kt == MLPT - 1))
                    p.edge("tensor", "vector")
                    v.tensor_tensor(zt2[j % 2][:], p.ps(bank)[:, 0:NB], wb[:, b2 * NB:(b2 + 1) * NB], ALU.mult)
                    v.tensor_tensor(moacc[:, ot * HALF + b2 * NB:ot * HALF + (b2 + 1) * NB],
                                    moacc[:, ot * HALF + b2 * NB:ot * HALF + (b2 + 1) * NB],
                                    zt2[j % 2][:], ALU.add)
                    msig.append(p.sig("vector"))
        p.edge("vector", "scalar")
        for ct in range(DT):
            s.activation(moacc[:, ct * HALF:(ct + 1) * HALF], moacc[:, ct * HALF:(ct + 1) * HALF], AF.Silu)
        p.edge("scalar", "vector")
        for ct in range(DT):
            p.wait("sync", "vE")
            p.dma("sync", aux1[:, 0:HALF], io["s1_s"][ct * 128:(ct + 1) * 128, tok0:tok0 + HALF])
            p.dma("sync", aux2[:, 0:HALF], io["xb1_s"][ct * 128:(ct + 1) * 128, tok0:tok0 + HALF])
            p.wait("vector", "dS")
            v.tensor_tensor(moacc[:, ct * HALF:(ct + 1) * HALF], moacc[:, ct * HALF:(ct + 1) * HALF],
                            aux1[:, 0:HALF], ALU.mult)
            p.vsync()
            v.tensor_tensor(moacc[:, ct * HALF:(ct + 1) * HALF], moacc[:, ct * HALF:(ct + 1) * HALF],
                            aux2[:, 0:HALF], ALU.add)
            p.vsync()
        for tt in range(8):
            p.edge("vector", "tensor")
            for ct in range(DT):
                t.transpose(p.ps(4 + ct % 2)[:, 0:128],
                            moacc[:, ct * HALF + tt * 128:ct * HALF + (tt + 1) * 128], ident[:])
                p.edge("tensor", "vector")
                v.tensor_copy(otm[:, ct * 128:(ct + 1) * 128], p.ps(4 + ct % 2)[:, 0:128])
                p.vsync()
                p.edge("vector", "tensor")
            p.sig("vector")
            p.wait("sync", "vE")
            p.dma("sync", io["y"][tok0 + tt * 128:tok0 + (tt + 1) * 128, :], otm[:])
            p.wait("vector", "dS")
    p.wait("gpsimd", "dS")
    p.wait("gpsimd", "dG")


PHASES["F6b"] = dict(
    emit=emit_F6b,
    inputs=["preT", "gateT", "postT"],
    consts=["onescol", "onesrow", "ident", "rankhot"],
    scratch_r=["ggath_s", "xn2b_s", "xb1_s", "s1_s"],
    scratch_w=["y"],
)
SCRATCH["y"] = ([TC, D], F32)
INPUT_SHAPES["rankhot"] = ([128, NCORES], F32)
